# version 19
# baseline (speedup 1.0000x reference)
import numpy as np
import ml_dtypes
from contextlib import ExitStack

import concourse.bass as bass
import concourse.mybir as mybir
from concourse.bass_utils import run_bass_kernel_spmd

F32 = mybir.dt.float32
BF16 = mybir.dt.bfloat16
AF = mybir.ActivationFunctionType
ALU = mybir.AluOpType
AX = mybir.AxisListType
BF = ml_dtypes.bfloat16

B, T, D, E, HD, DK = 32, 2048, 1024, 16, 64, 128
NCORES = 8
SPC = B // NCORES
NT = T // 128
NCH = D // 128
GRP = 4
BIGV = 1.0e30
LN_EPS = 1.0e-5
INV_SQRT_DK = float(1.0 / np.sqrt(DK))


class _Step:

    def __init__(self, eng, sem, base):
        self.eng, self.sem, self.c = eng, sem, base

    def __call__(self, r):
        self.c += 1
        r.then_inc(self.sem, 1)
        self.eng.wait_ge(self.sem, self.c)
        return r


HOP_NS = 300.0


class Sched:
    STREAM = {"pe": "tensor", "act": "scalar", "dve": "vector", "pool": "gpsimd",
              "qin0": "sync", "qin1": "sync", "qout0": "sync", "qout1": "sync",
              "qw0": "gpsimd", "qw1": "gpsimd"}
    DMA = {"qin0", "qin1", "qout0", "qout1", "qw0", "qw1"}
    QSPLIT = {"qin": 2, "qout": 2, "qw": 2}
    STRICT = {"vector", "scalar", "gpsimd"}
    STREAMS = ("tensor", "scalar", "vector", "gpsimd", "sync")

    def __init__(self, sems):
        self.sems = sems
        self.rec = []
        self.writer = {}
        self.readers = {}
        self.qrr = {}

    def add(self, peng, fn, reads=(), writes=(), n_inc=1, stepped=None,
            cost=500.0, xfer=0.0):
        if peng in self.QSPLIT:
            k = self.qrr.get(peng, 0)
            self.qrr[peng] = k + 1
            peng = f"{peng}{k % self.QSPLIT[peng]}"
        deps = set()
        for b in reads:
            w = self.writer.get(b)
            if w is not None:
                deps.add(w)
        for b in writes:
            w = self.writer.get(b)
            if w is not None:
                deps.add(w)
            deps.update(self.readers.get(b, ()))
        i = len(self.rec)
        if peng == "dve":
            cost = float(cost) + 250.0 * max(1, n_inc)
        elif peng == "act":
            cost = float(cost) + 150.0 * max(1, n_inc)
        self.rec.append(dict(
            i=i, peng=peng, fn=fn, n_inc=n_inc,
            stepped=(n_inc > 1 if stepped is None else stepped),
            cost=float(cost), xfer=float(xfer), deps=deps,
            reads=tuple(reads), writes=tuple(writes)))
        for b in writes:
            self.writer[b] = i
            self.readers[b] = set()
        for b in reads:
            self.readers.setdefault(b, set()).add(i)
        return i

    def schedule(self, eps=2000.0):
        n = len(self.rec)
        children = [[] for _ in range(n)]
        indeg = [0] * n
        for r in self.rec:
            for d in r["deps"]:
                children[d].append(r["i"])
                indeg[r["i"]] += 1
        rank = [0.0] * n
        for i in range(n - 1, -1, -1):
            r = self.rec[i]
            mx = 0.0
            for c in children[i]:
                if rank[c] > mx:
                    mx = rank[c]
            rank[i] = mx + r["cost"] + r["xfer"]
        finish = [0.0] * n
        rt = [0.0] * n
        stream_free = {st: 0.0 for st in self.STREAMS}
        readyset = set(i for i in range(n) if indeg[i] == 0)
        order = []
        while len(order) < n:
            est_min = None
            ests = {}
            for i in readyset:
                r = self.rec[i]
                est = stream_free[self.STREAM[r["peng"]]]
                if rt[i] > est:
                    est = rt[i]
                ests[i] = est
                if est_min is None or est < est_min:
                    est_min = est
            best = None
            best_key = None
            for i, est in ests.items():
                if est <= est_min + eps:
                    key = (-rank[i], est, i)
                    if best_key is None or key < best_key:
                        best_key = key
                        best = i
            i = best
            readyset.discard(i)
            r = self.rec[i]
            st = self.STREAM[r["peng"]]
            start = ests[i]
            stream_free[st] = start + r["cost"]
            finish[i] = start + r["cost"] + r["xfer"]
            order.append(i)
            for c in children[i]:
                hop = HOP_NS if self.STREAM[self.rec[c]["peng"]] != st else 0.0
                v = finish[i] + hop
                if v > rt[c]:
                    rt[c] = v
                indeg[c] -= 1
                if indeg[c] == 0:
                    readyset.add(c)
        self.order = order
        self.makespan = max(finish) if n else 0.0
        return order

    def gen_streams(self):
        cnt = {p: 0 for p in self.STREAM}
        writer = {}
        readers = {}
        mark = {}
        out = {st: [] for st in self.STREAMS}
        for i in self.order:
            r = self.rec[i]
            peng = r["peng"]
            stream = self.STREAM[peng]
            deps = {}
            def dep(p, c):
                if c > deps.get(p, 0):
                    deps[p] = c
            for b in r["reads"]:
                w = writer.get(b)
                if w is not None:
                    dep(*w)
            for b in r["writes"]:
                w = writer.get(b)
                if w is not None:
                    dep(*w)
                for p, c in readers.get(b, {}).items():
                    dep(p, c)
            waits = []
            for p, c in sorted(deps.items()):
                if p == peng:
                    continue
                assert c <= cnt[p], f"forward dep {p}:{c} > {cnt[p]}"
                if p in self.DMA:
                    c = cnt[p]
                key = (stream, p)
                if mark.get(key, 0) >= c:
                    continue
                mark[key] = c
                waits.append((self.sems[p], 16 * c if p in self.DMA else c))
            base = cnt[peng]
            cnt[peng] += r["n_inc"]
            cn = cnt[peng]
            out[stream].append((waits, r["fn"], self.sems[peng],
                                16 if peng in self.DMA else 1, r["n_inc"], base,
                                r["stepped"]))
            for b in r["writes"]:
                writer[b] = (peng, cn)
                readers[b] = {}
            for b in r["reads"]:
                readers.setdefault(b, {})[peng] = cn
        self.cnt = cnt
        return out


def build_program(n_seq=SPC, n_sub=3, dump=(), il_b1=False, il_b3=False):
    nc = bass.Bass("TRN2", target_bir_lowering=False)

    xin = nc.dram_tensor("xin", [SPC * T, D], F32, kind="ExternalInput")
    xout = nc.dram_tensor("xout", [SPC * T, D], F32, kind="ExternalOutput")
    wq_d = nc.dram_tensor("wq", [D, E * HD], F32, kind="ExternalInput")
    wk_d = nc.dram_tensor("wk", [D, E * HD], F32, kind="ExternalInput")
    wv_d = nc.dram_tensor("wv", [D, E * HD], F32, kind="ExternalInput")
    wo_d = nc.dram_tensor("wo", [E * HD, D], F32, kind="ExternalInput")
    wqt_d = nc.dram_tensor("wqt", [D, DK], F32, kind="ExternalInput")
    wkt_d = nc.dram_tensor("wkt", [D, DK], F32, kind="ExternalInput")
    simr_d = nc.dram_tensor("simr", [128, 128], F32, kind="ExternalInput")
    gates_d = nc.dram_tensor("gates", [1, E], F32, kind="ExternalInput")
    gamma_d = nc.dram_tensor("gamma", [1, D], F32, kind="ExternalInput")
    beta_d = nc.dram_tensor("beta", [1, D], F32, kind="ExternalInput")
    ident_d = nc.dram_tensor("ident", [128, 128], BF16, kind="ExternalInput")
    tri_d = nc.dram_tensor("trimask", [128, 128], BF16, kind="ExternalInput")
    ones_d = nc.dram_tensor("consts", [128, 4], F32, kind="ExternalInput")
    dscr = nc.dram_tensor("dscr", [1, 64], F32)

    with ExitStack() as st:
        sb = lambda n, sh, dt: st.enter_context(nc.sbuf_tensor(n, sh, dt))
        psm = lambda n, sh, dt: st.enter_context(nc.psum_tensor(n, sh, dt))
        sem = lambda n: st.enter_context(nc.semaphore(n))

        wq = sb("wq_s", [128, NCH * 1024], BF16)
        wk = sb("wk_s", [128, NCH * 1024], BF16)
        wv = sb("wv_s", [128, NCH * 1024], BF16)
        wo = sb("wo_s", [128, NCH * 1024], BF16)
        wqt = sb("wqt_s", [128, NCH * 128], BF16)
        wkt = sb("wkt_s", [128, NCH * 128], BF16)
        simn = sb("simn_s", [128, NCH * 16], BF16)

        sgb = sb("sgb_s", [128, E], F32)
        gmb = sb("gmb_s", [128, D], BF16)
        btb = sb("btb_s", [128, D], BF16)
        ident = sb("ident_s", [128, 128], BF16)
        tri = sb("tri_s", [128, 128], BF16)
        consts = sb("consts_s", [128, 4], F32)
        x = sb("x_s", [128, NT * 1024], F32)
        lnT = [sb(f"lnT{i}", [128, GRP * 1024], BF16) for i in range(2)]
        lns = [sb(f"lns{i}", [128, 1024], BF16) for i in range(2)]
        prod = sb("prod_s", [128, 1024], F32)
        s_cn = prod[0:1, 256:512]
        s_rs = prod[0:1, 512:544]
        Macc = sb("Macc_s", [128, 1024], F32)
        qtok = [sb(f"qtok{i}", [128, 128], BF16) for i in range(2)]
        vsb = sb("v_s", [128, NT * 64], BF16)
        qTs = sb("qT_s", [64, T], BF16)
        kTs = sb("kT_s", [64, T], BF16)
        Psb_ = sb("P_s0", [128, T], BF16)
        Psb = [Psb_, Psb_]
        PTs = sb("PT_s", [128, T], BF16)
        attn = [sb(f"attn{i}", [128, 64], BF16) for i in range(2)]
        mixed = sb("mixed_s", [128, 1024], BF16)
        mixT = sb("mixT_s", [128, 1024], BF16)
        lnTa = mixT
        lg = [sb(f"lg{i}", [128, 16 * GRP], F32) for i in range(4)]
        rw = [sb(f"rw{i}", [128, 16 * GRP], F32) for i in range(3)]
        mv = [sb(f"mv{i}", [128, 2 * GRP], F32) for i in range(4)]
        rstdg = [sb(f"rstdg{i}", [128, GRP], F32) for i in range(4)]
        biasg = [sb(f"biasg{i}", [128, GRP], F32) for i in range(4)]
        invng = [sb(f"invng{i}", [128, GRP], F32) for i in range(4)]
        stg = [sb(f"stg{i}", [128, 4 * GRP], F32) for i in range(4)]
        bn6 = [sb(f"bn6{i}", [128, 12], F32) for i in range(4)]
        g_gt = sb("g_gt", [128, 16 * GRP], F32)
        g_ps = sb("g_ps", [128, 16 * GRP], F32)
        g_i1 = sb("g_i1", [128, 16 * GRP], F32)
        g_t2 = sb("g_t2", [128, 16 * GRP], F32)
        g_m = sb("g_m", [128, 16 * GRP], F32)
        s_np = sb("s_np", [128, GRP], F32)
        s_ac = sb("s_ac", [128, GRP], F32)
        s_in = sb("s_in", [128, GRP], F32)
        s_m1 = sb("s_m1", [128, GRP], F32)
        s_m2 = sb("s_m2", [128, GRP], F32)
        QTs = sb("QT_s", [128, T], BF16)
        K2 = [sb(f"K2{i}", [128, 128], BF16) for i in range(2)]
        Msb = sb("M_s", [128, 1024], BF16)
        Jn = [sb(f"Jn_s{i}", [128, 1024], F32) for i in range(2)]
        r2a = sb("r2a_s", [128, NT], F32)
        b2a = sb("b2a_s", [128, NT], F32)
        mv3 = [sb(f"mv3_s{i}", [128, 2], F32) for i in range(2)]
        s_t3 = [sb(f"s_t3{i}", [128, 2], F32) for i in range(2)]
        s_b3 = [sb(f"s_b3{i}", [128, 1], F32) for i in range(2)]
        s_nmA = sb("s_nmA", [128, 1], F32)
        s_nmB = sb("s_nmB", [128, 1], F32)
        s_nm = sb("s_nm", [128, 1], F32)
        s_nm8 = [sb(f"s_nm8_{i}", [128, 1], F32) for i in range(2)]
        s_suA = sb("s_suA", [128, 1], F32)
        s_suB = sb("s_suB", [128, 1], F32)
        s_sum = sb("s_sum", [128, 1], F32)
        s_rcp = [sb(f"s_rcp{i}", [128, 1], F32) for i in range(2)]
        s_nmp = sb("s_nmp", [128, 4], F32)
        s_sup = [sb(f"s_sup{i}", [128, 4], F32) for i in range(2)]
        s_sg = sb("s_sg", [1, 64], F32)
        s_rsb = sb("s_rsb", [128, 16], F32)

        pA = psm("pA", [128, 1024], F32)
        pB = psm("pB", [128, 1024], F32)
        pC = psm("pC", [128, 1024], F32)
        pT = psm("pT", [128, 1024], BF16)
        pS = psm("pS", [128, 512], F32)

        sems = {p: sem(f"sem_{p}") for p in Sched.STREAM}
        S = Sched(sems)

        def bankkeys(name, lo, hi):
            ks = []
            if lo < 512:
                ks.append(name + "0")
            if hi > 512:
                ks.append(name + "1")
            return ks

        S.add("qin", lambda e: e.dma_start(ident[:, :], ident_d.ap()), writes=["ident"])
        S.add("qin", lambda e: e.dma_start(tri[:, :], tri_d.ap()), writes=["tri"])
        S.add("qin", lambda e: e.dma_start(consts[:, :], ones_d.ap()), writes=["ones"])
        S.add("qin", lambda e: e.dma_start(s_sg[0:1, 0:16], gates_d.ap()), writes=["sg_raw"])
        simr = prod[:, 128:256]
        S.add("qin", lambda e: e.dma_start(simr, simr_d.ap()), writes=["prod0"])
        for t in range(NT):
            S.add("qin", lambda e, t=t: e.dma_start(
                x[:, 1024 * t:1024 * (t + 1)], xin[128 * t:128 * (t + 1), :]),
                writes=[f"x{t}"], cost=300, xfer=4000)

        for w_s, w_d, key in ((wq, wq_d, "wq"), (wk, wk_d, "wk"), (wv, wv_d, "wv"),
                              (wo, wo_d, "wo"), (wqt, wqt_d, "wqt"), (wkt, wkt_d, "wkt")):
            S.add("qw", lambda e, w_s=w_s, w_d=w_d: e.dma_start(
                w_s[:, :].rearrange("p (c n) -> p c n", c=NCH),
                w_d.ap().rearrange("(c p) n -> p c n", p=128)), writes=[key],
                cost=1200, xfer=16000)
        S.add("qw", lambda e: e.dma_start(
            gmb[:, :], bass.AP(gamma_d, 0, [[0, 128], [1, D]])), writes=["gmb"],
            cost=1000, xfer=4000)
        S.add("qw", lambda e: e.dma_start(
            btb[:, :], bass.AP(beta_d, 0, [[0, 128], [1, D]])), writes=["btb"],
            cost=1000, xfer=4000)

        S.add("act", lambda e: e.activation(s_sg[0:1, 16:32], s_sg[0:1, 0:16], AF.Exp,
                                            scale=-1.0), reads=["sg_raw"], writes=["sg_e"])
        S.add("dve", lambda e: e.tensor_scalar(s_sg[0:1, 32:48], s_sg[0:1, 16:32],
                                               1.0, None, ALU.add),
              reads=["sg_e"], writes=["sg_v"])
        S.add("dve", lambda e: e.reciprocal(s_sg[0:1, 48:64], s_sg[0:1, 32:48]),
              reads=["sg_v"], writes=["sg_r"])
        S.add("qout", lambda e: e.dma_start(dscr[0:1, 0:16], s_sg[0:1, 48:64]),
              reads=["sg_r"], writes=["dscr"])
        S.add("qin", lambda e: e.dma_start(sgb[:, :], bass.AP(dscr, 0, [[0, 128], [1, 16]])),
              reads=["dscr"], writes=["sgb"])

        S.add("dve", lambda e: e.tensor_tensor(prod[:, 0:128], simr, simr,
                                               ALU.mult), reads=["prod0"], writes=["prod0"])
        S.add("pe", lambda e: e.matmul(pS[0:1, 0:128], consts[:, 0:1], prod[:, 0:128],
                                       start=True, stop=True),
              reads=["ones", "prod0"], writes=["pS"])
        def cn_fn(e, stp):
            stp(e.tensor_copy(prod[0:1, 256:384], pS[0:1, 0:128]))
            stp(e.tensor_tensor(prod[0:1, 384:448], s_cn[0:1, 0:64], s_cn[0:1, 64:128], ALU.add))
            stp(e.tensor_tensor(prod[0:1, 448:480], s_cn[0:1, 128:160], s_cn[0:1, 160:192], ALU.add))
            stp(e.tensor_tensor(prod[0:1, 480:496], s_cn[0:1, 192:208], s_cn[0:1, 208:224], ALU.add))
            stp(e.tensor_scalar(prod[0:1, 496:512], prod[0:1, 480:496], 1.0e-18, None,
                                ALU.max))
        S.add("dve", cn_fn, reads=["pS"], writes=["prod0"], n_inc=5)
        def rs_fn(e, stp):
            stp(e.activation(prod[0:1, 512:528], prod[0:1, 496:512], AF.Ln))
            stp(e.activation(prod[0:1, 528:544], prod[0:1, 512:528], AF.Exp, scale=-0.5))
        S.add("act", rs_fn, reads=["prod0"], writes=["prod1"], n_inc=2)
        S.add("qout", lambda e: e.dma_start(dscr[0:1, 16:32], prod[0:1, 528:544]),
              reads=["prod1"], writes=["dscr2"])
        S.add("qin", lambda e: e.dma_start(s_rsb[:, :],
                                           bass.AP(dscr, 16, [[0, 128], [1, 16]])),
              reads=["dscr2"], writes=["rsb"])
        S.add("dve", lambda e: e.tensor_tensor(
            simn[:, :].rearrange("p (c e) -> p c e", c=NCH),
            simr.rearrange("p (c e) -> p c e", c=NCH),
            s_rsb[:, :].unsqueeze(1).broadcast_to([128, NCH, 16]), ALU.mult),
            reads=["prod0", "rsb"], writes=["simn"])

        def ln_group_stats(g, sl, with_invn, rstd_ap, bias_ap, wkeys):
            for u in range(GRP):
                t = g * GRP + u
                def f(e, stp, t=t, u=u, sl=sl, us=u % 2):
                    stp(e.activation(x[:, 1024 * t:1024 * (t + 1)],
                                     x[:, 1024 * t:1024 * (t + 1)], AF.Identity,
                                     accum_out=mv[sl][:, 2 * u:2 * u + 1]))
                    stp(e.activation(lns[us][:, :], x[:, 1024 * t:1024 * (t + 1)],
                                     AF.Square,
                                     accum_out=mv[sl][:, 2 * u + 1:2 * u + 2]))
                S.add("act", f, reads=[f"x{t}"], writes=[f"mv{sl}", f"lns{u % 2}"],
                      n_inc=2, cost=1900)
            sxv = mv[sl][:, 0:2 * GRP:2]
            sqv = mv[sl][:, 1:2 * GRP:2]
            def fv(e, stp, sl=sl, sxv=sxv, sqv=sqv):
                stp(e.scalar_tensor_tensor(stg[sl][:, 3 * GRP:4 * GRP], sxv,
                                           -1.0 / (D * D), sxv, ALU.mult, ALU.mult))
                stp(e.scalar_tensor_tensor(stg[sl][:, 2 * GRP:3 * GRP], sqv, 1.0 / D,
                                           stg[sl][:, 3 * GRP:4 * GRP],
                                           ALU.mult, ALU.add))
            S.add("dve", fv, reads=[f"mv{sl}"], writes=[f"stg{sl}"], n_inc=2, cost=500)
            varv = stg[sl][:, 2 * GRP:3 * GRP]
            def fb(e, stp, sl=sl, varv=varv, rstd_ap=rstd_ap, with_invn=with_invn):
                stp(e.activation(stg[sl][:, 0:GRP], varv, AF.Ln, bias=consts[:, 1:2]))
                stp(e.activation(rstd_ap, stg[sl][:, 0:GRP], AF.Exp, scale=-0.5))
                if with_invn:
                    stp(e.activation(stg[sl][:, GRP:2 * GRP], varv, AF.Ln, scale=float(D),
                                     bias=consts[:, 2:3]))
                    stp(e.activation(invng[sl][:, :], stg[sl][:, GRP:2 * GRP], AF.Exp,
                                     scale=-0.5))
            S.add("act", fb, reads=[f"stg{sl}", "ones"],
                  writes=[wkeys[0], f"stg{sl}"] + ([f"invng{sl}"] if with_invn else []),
                  n_inc=4 if with_invn else 2, cost=1000)
            S.add("dve", lambda e, sl=sl, rstd_ap=rstd_ap, bias_ap=bias_ap:
                  e.scalar_tensor_tensor(bias_ap, mv[sl][:, 0:2 * GRP:2], -1.0 / D,
                                         rstd_ap, ALU.mult, ALU.mult),
                  reads=[f"mv{sl}", wkeys[0]], writes=[wkeys[1]], cost=250)

        def ln_apply(t, u, scale_ap, bias_ap, rkeys):
            us = u % 2
            S.add("act", lambda e: e.activation(
                lns[us][:, :], x[:, 1024 * t:1024 * (t + 1)], AF.Identity,
                scale=scale_ap, bias=bias_ap),
                reads=[f"x{t}"] + rkeys, writes=[f"lns{us}"], cost=1100)

        def transpose8(src, src_key, dst_evac, dst_keys):
            def f(e):
                last = None
                for c in range(8):
                    last = e.transpose(pT[:, 128 * c:128 * (c + 1)],
                                       src[:, 128 * c:128 * (c + 1)], ident[:, :])
                return last
            S.add("pe", f, reads=[src_key, "ident"], writes=["pT"], cost=600)
            S.add("act", dst_evac, reads=["pT"], writes=dst_keys, cost=1100)

        pSb = pS[:, :].bitcast(BF16)

        def emit_ffn(s, second, group_pre=None, group_post=None):
            for g in range(NT // GRP):
                sl = g % 4
                rsl = g % 3
                if group_pre is not None:
                    group_pre(g)
                ln_group_stats(g, sl, True, rstdg[sl][:, :], biasg[sl][:, :],
                               [f"rstdg{sl}", f"biasg{sl}"])
                for u in range(GRP):
                    t = g * GRP + u
                    us = u % 2
                    ln_apply(t, u, rstdg[sl][:, u:u + 1], biasg[sl][:, u:u + 1],
                             [f"rstdg{sl}", f"biasg{sl}"])
                    transpose8(lns[us], f"lns{us}",
                               (lambda e, sl=sl, u=u: e.copy(
                                   lnT[sl % 2][:, 1024 * u:1024 * (u + 1)], pT[:, :])),
                               [f"lnT{sl % 2}"])
                    def fg(e, sl=sl, u=u):
                        last = None
                        for c in range(NCH):
                            last = e.matmul(pS[:, 0:16],
                                            lnT[sl % 2][:, 1024 * u + 128 * c:1024 * u + 128 * (c + 1)],
                                            simn[:, 16 * c:16 * (c + 1)],
                                            start=(c == 0), stop=(c == NCH - 1))
                        return last
                    S.add("pe", fg, reads=[f"lnT{sl % 2}", "simn"], writes=["pS"], cost=600)
                    S.add("dve", lambda e, sl=sl, u=u: e.scalar_tensor_tensor(
                        lg[sl][:, 16 * u:16 * (u + 1)], pS[:, 0:16],
                        invng[sl][:, u:u + 1], sgb[:, :], ALU.mult, ALU.subtract),
                        reads=["pS", f"invng{sl}", "sgb"], writes=[f"lg{sl}"], cost=250)
                def epi(e, stp, sl=sl):
                    lgv = lg[sl][:, :]
                    lg3 = lgv.rearrange("p (s e) -> p s e", e=16)
                    def bc(a):
                        return a.unsqueeze(2).broadcast_to([128, GRP, 16])
                    def r3(a):
                        return a.rearrange("p (s e) -> p s e", e=16)
                    stp(e.tensor_scalar(g_gt[:, :], lgv, 0.0, None, ALU.max))
                    stp(e.tensor_scalar(g_ps[:, :], lgv, 0.0, None, ALU.is_gt))
                    stp(e.tensor_reduce(s_np[:, :], r3(g_ps[:, :]), AX.X, ALU.add))
                    stp(e.tensor_scalar(s_ac[:, :], s_np[:, :], 0.0, None, ALU.is_gt))
                    stp(e.tensor_scalar(s_in[:, :], s_ac[:, :], -1.0, 1.0, ALU.mult,
                                        ALU.add))
                    stp(e.tensor_reduce(s_m1[:, :], lg3, AX.X, ALU.max))
                    stp(e.tensor_tensor(r3(g_i1[:, :]), lg3, bc(s_m1[:, :]), ALU.is_ge))
                    stp(e.scalar_tensor_tensor(g_m[:, :], g_i1[:, :], -BIGV, lgv,
                                               ALU.mult, ALU.add))
                    stp(e.tensor_reduce(s_m2[:, :], r3(g_m[:, :]), AX.X, ALU.max))
                    stp(e.tensor_tensor(r3(g_t2[:, :]), lg3, bc(s_m2[:, :]), ALU.is_ge))
                    stp(e.tensor_tensor(r3(g_i1[:, :]), r3(g_ps[:, :]), bc(s_ac[:, :]),
                                        ALU.mult))
                    stp(e.tensor_tensor(r3(g_m[:, :]), r3(g_t2[:, :]), bc(s_in[:, :]),
                                        ALU.mult))
                    stp(e.tensor_tensor(g_ps[:, :], g_i1[:, :], g_m[:, :], ALU.add))
                    stp(e.tensor_scalar(g_m[:, :], g_ps[:, :], -1.0, BIGV, ALU.add,
                                        ALU.mult))
                    stp(e.tensor_tensor(g_t2[:, :], g_m[:, :], g_gt[:, :], ALU.add))
                    stp(e.tensor_reduce(s_m1[:, :], r3(g_t2[:, :]), AX.X, ALU.max))
                    stp(e.tensor_tensor(r3(g_ps[:, :]), r3(g_t2[:, :]), bc(s_m1[:, :]),
                                        ALU.subtract))
                S.add("dve", epi, reads=[f"lg{sl}", "sgb"],
                      writes=["g_gt", "g_ps", "g_i1", "g_t2", "g_m",
                              "s_np", "s_ac", "s_in", "s_m1", "s_m2"], n_inc=17,
                      cost=3500)
                S.add("act", lambda e: e.activation(g_i1[:, :], g_ps[:, :], AF.Exp),
                      reads=["g_ps"], writes=["g_i1"], cost=300)
                def epi2(e, stp, rsl=rsl):
                    def bc(a):
                        return a.unsqueeze(2).broadcast_to([128, GRP, 16])
                    def r3(a):
                        return a.rearrange("p (s e) -> p s e", e=16)
                    stp(e.tensor_reduce(s_m2[:, :], r3(g_i1[:, :]), AX.X, ALU.add))
                    stp(e.reciprocal(s_np[:, :], s_m2[:, :]))
                    stp(e.tensor_tensor(r3(rw[rsl][:, :]), r3(g_i1[:, :]),
                                        bc(s_np[:, :]), ALU.mult))
                S.add("dve", epi2, reads=["g_i1"],
                      writes=[f"rw{rsl}", "s_m2", "s_np"], n_inc=3, cost=700)

                for u in range(GRP):
                    t = g * GRP + u
                    us = u % 2
                    for nm, pX, pk, w_s, wkey in (("q", pA, "pA", wq, "wq"),
                                                  ("k", pB, "pB", wk, "wk"),
                                                  ("v", pC, "pC", wv, "wv")):
                        for h in range(2):
                            def fp(e, sl=sl, u=u, pX=pX, w_s=w_s, h=h):
                                last = None
                                for c in range(NCH):
                                    last = e.matmul(
                                        pX[:, 512 * h:512 * (h + 1)],
                                        lnT[sl % 2][:, 1024 * u + 128 * c:1024 * u + 128 * (c + 1)],
                                        w_s[:, 1024 * c + 512 * h:1024 * c + 512 * (h + 1)],
                                        start=(c == 0), stop=(c == NCH - 1))
                                return last
                            S.add("pe", fp, reads=[f"lnT{sl % 2}", wkey],
                                  writes=[pk + str(h)], cost=1800)
                        for h in range(2):
                            S.add("dve", lambda e, pX=pX, rsl=rsl, u=u, h=h:
                                  e.tensor_tensor(
                                prod[:, 512 * h:512 * (h + 1)]
                                .rearrange("p (ex d) -> p ex d", ex=8),
                                pX[:, 512 * h:512 * (h + 1)]
                                .rearrange("p (ex d) -> p ex d", ex=8),
                                rw[rsl][:, 16 * u + 8 * h:16 * u + 8 * (h + 1)]
                                .unsqueeze(2).broadcast_to([128, 8, 64]), ALU.mult),
                                reads=[pk + str(h), f"rw{rsl}"],
                                writes=["prod" + str(h)], cost=750)
                        if nm == "q":
                            dst, dkey = qtok[us][:, 0:64], f"qtok{us}"
                        elif nm == "k":
                            dst, dkey = qtok[us][:, 64:128], f"qtok{us}"
                        else:
                            dst, dkey = vsb[:, 64 * t:64 * (t + 1)], f"v{t}"
                        def fred(e, dst=dst):
                            with nc.allow_low_precision(reason="fp32 red bf16 out"):
                                return e.tensor_reduce(
                                    dst, prod[:, :].rearrange("p (ex d) -> p ex d", ex=16)
                                    .transpose([0, 2, 1]), AX.X, ALU.add)
                        S.add("dve", fred, reads=["prod0", "prod1"], writes=[dkey],
                              cost=1300)

                    def ft(e, us=us):
                        e.transpose(pSb[0:64, 0:128], qtok[us][:, 0:64], ident[:, :])
                        return e.transpose(pSb[0:64, 128:256], qtok[us][:, 64:128],
                                           ident[:, :])
                    S.add("pe", ft, reads=[f"qtok{us}", "ident"], writes=["pS"], cost=300)
                    def fte(e, stp, t=t):
                        stp(e.copy(qTs[:, 128 * t:128 * (t + 1)], pSb[0:64, 0:128]))
                        stp(e.copy(kTs[:, 128 * t:128 * (t + 1)], pSb[0:64, 128:256]))
                    S.add("act", fte, reads=["pS"], writes=[f"qT{t}", f"kT{t}"],
                          n_inc=2, cost=700)

                    L = 128 * (t + 1)
                    npieces = (L + 511) // 512
                    for j in range(npieces):
                        n = min(512, L - 512 * j)
                        pX = pA if j < 2 else pB
                        lo = 512 * (j % 2)
                        pkname = ("pA" if j < 2 else "pB") + str(j % 2)
                        kkeys = [f"kT{kk}" for kk in
                                 range(4 * j, min(4 * j + (n + 127) // 128, t + 1))]
                        is_diag = (j == npieces - 1)
                        def fs(e, pX=pX, lo=lo, n=n, j=j, t=t, is_diag=is_diag):
                            last = e.matmul(pX[:, lo:lo + n], qTs[:, 128 * t:128 * (t + 1)],
                                            kTs[:, 512 * j:512 * j + n],
                                            start=True, stop=not is_diag)
                            if is_diag:
                                moff = lo + (128 * t - 512 * j)
                                last = e.matmul(pX[:, moff:moff + 128], ident[:, :],
                                                tri[:, :], start=False, stop=True)
                            return last
                        S.add("pe", fs, reads=[f"qT{t}"] + kkeys
                              + (["ident", "tri"] if is_diag else []), writes=[pkname],
                              cost=100 + n // 2)
                    La = min(L, 1024)
                    Lb = L - La
                    tp = t % 2
                    def fm(e, stp, La=La, Lb=Lb, tp=tp):
                        if Lb > 0:
                            stp(e.tensor_reduce(s_nmp[:, 0:1], pA[:, 0:La], AX.X, ALU.max,
                                                negate=True))
                            stp(e.tensor_reduce(s_nmp[:, 1:2], pB[:, 0:Lb], AX.X, ALU.max,
                                                negate=True))
                            stp(e.tensor_tensor(s_nm[:, :], s_nmp[:, 0:1], s_nmp[:, 1:2],
                                                ALU.min))
                        else:
                            stp(e.tensor_reduce(s_nm[:, :], pA[:, 0:La], AX.X, ALU.max,
                                                negate=True))
                        stp(e.tensor_scalar(s_nm8[tp][:, :], s_nm[:, :], 0.125, None,
                                            ALU.mult))
                    S.add("dve", fm, reads=bankkeys("pA", 0, La) + bankkeys("pB", 0, Lb),
                          writes=["s_nmp", "s_nm", f"s_nm8_{tp}"],
                          n_inc=4 if Lb > 0 else 2, stepped=True, cost=500 + L)
                    S.add("act", lambda e, La=La, tp=tp: e.activation(
                        Psb[tp][:, 0:La], pA[:, 0:La], AF.Exp, scale=0.125,
                        bias=s_nm8[tp][:, :], accum_out=s_sup[tp][:, 0:1]),
                        reads=bankkeys("pA", 0, La) + [f"s_nm8_{tp}"],
                        writes=["P", f"s_sup{tp}"], cost=300 + La)
                    if Lb > 0:
                        S.add("act", lambda e, Lb=Lb, tp=tp: e.activation(
                            Psb[tp][:, 1024:1024 + Lb], pB[:, 0:Lb], AF.Exp, scale=0.125,
                            bias=s_nm8[tp][:, :], accum_out=s_sup[tp][:, 1:2]),
                            reads=bankkeys("pB", 0, Lb) + [f"s_nm8_{tp}"],
                            writes=["P", f"s_sup{tp}"], cost=300 + Lb)
                    def fr(e, stp, Lb=Lb, tp=tp):
                        if Lb > 0:
                            stp(e.tensor_reduce(s_sum[:, :], s_sup[tp][:, 0:2], AX.X,
                                                ALU.add))
                            stp(e.reciprocal(s_rcp[tp][:, :], s_sum[:, :]))
                        else:
                            stp(e.reciprocal(s_rcp[tp][:, :], s_sup[tp][:, 0:1]))
                    S.add("dve", fr, reads=[f"s_sup{tp}"],
                          writes=["s_sum", f"s_rcp{tp}"],
                          n_inc=2 if Lb > 0 else 1, stepped=True, cost=450)
                    for r in range((t + 1 + 7) // 8):
                        kcnt = min(8, t + 1 - 8 * r)
                        def fpt(e, r=r, kcnt=kcnt, tp=tp):
                            last = None
                            for kk in range(kcnt):
                                last = e.transpose(
                                    pT[:, 128 * kk:128 * (kk + 1)],
                                    Psb[tp][:, 128 * (8 * r + kk):128 * (8 * r + kk + 1)],
                                    ident[:, :])
                            return last
                        S.add("pe", fpt, reads=["P", "ident"], writes=["pT"],
                              cost=100 + 55 * kcnt)
                        S.add("act", lambda e, r=r, kcnt=kcnt: e.copy(
                            PTs[:, 1024 * r:1024 * r + 128 * kcnt], pT[:, 0:128 * kcnt]),
                            reads=["pT"], writes=["PT"], cost=250 + 107 * kcnt)
                    def fpv(e, t=t):
                        last = None
                        for kk in range(t + 1):
                            last = e.matmul(pS[:, 0:64], PTs[:, 128 * kk:128 * (kk + 1)],
                                            vsb[:, 64 * kk:64 * (kk + 1)],
                                            start=(kk == 0), stop=(kk == t))
                        return last
                    S.add("pe", fpv, reads=["PT"] + [f"v{kk}" for kk in range(t + 1)],
                          writes=["pS"], cost=100 + 55 * (t + 1))
                    S.add("act", lambda e, us=us, tp=tp: e.activation(
                        attn[us][:, :], pS[:, 0:64], AF.Identity, scale=s_rcp[tp][:, :]),
                        reads=["pS", f"s_rcp{tp}"], writes=[f"attn{us}"], cost=300)
                    S.add("pool", lambda e, us=us, u=u, rsl=rsl: e.tensor_tensor(
                        mixed[:, :].rearrange("p (ex d) -> p ex d", ex=16),
                        attn[us][:, :].unsqueeze(1).broadcast_to([128, 16, 64]),
                        rw[rsl][:, 16 * u:16 * (u + 1)].unsqueeze(2).broadcast_to(
                            [128, 16, 64]), ALU.mult),
                        reads=[f"attn{us}", f"rw{rsl}"], writes=["mixed"], cost=2300)
                    def fmt(e):
                        last = None
                        for c in range(8):
                            last = e.transpose(pSb[:, 128 * c:128 * (c + 1)],
                                               mixed[:, 128 * c:128 * (c + 1)],
                                               ident[:, :])
                        return last
                    S.add("pe", fmt, reads=["mixed", "ident"], writes=["pS"], cost=600)
                    S.add("act", lambda e: e.copy(mixT[:, :], pSb[:, 0:1024]),
                          reads=["pS"], writes=["mixT"], cost=1100)
                    for h in range(2):
                        def fo(e, h=h):
                            last = None
                            for c in range(NCH):
                                last = e.matmul(
                                    pC[:, 512 * h:512 * (h + 1)],
                                    mixT[:, 128 * c:128 * (c + 1)],
                                    wo[:, 1024 * c + 512 * h:1024 * c + 512 * (h + 1)],
                                    start=(c == 0), stop=(c == NCH - 1))
                            return last
                        S.add("pe", fo, reads=["mixT", "wo"], writes=["pC" + str(h)],
                              cost=1800)
                        S.add("dve", lambda e, t=t, h=h: e.scalar_tensor_tensor(
                            x[:, 1024 * t + 512 * h:1024 * t + 512 * (h + 1)],
                            pC[:, 512 * h:512 * (h + 1)], 0.5,
                            x[:, 1024 * t + 512 * h:1024 * t + 512 * (h + 1)],
                            ALU.mult, ALU.add),
                            reads=["pC" + str(h), f"x{t}"], writes=[f"x{t}"],
                            cost=750)
                if group_post is not None:
                    group_post(g)

        pT32 = pT[:, :].bitcast(F32)

        def attn_B1_group(g):
            asl = 2 + (g % 2)
            ln_group_stats(g, asl, False, r2a[:, GRP * g:GRP * (g + 1)],
                           b2a[:, GRP * g:GRP * (g + 1)], ["r2a", "b2a"])
            for u in range(GRP):
                t = g * GRP + u
                us = u % 2
                ln_apply(t, u, r2a[:, t:t + 1], b2a[:, t:t + 1], ["r2a", "b2a"])
                transpose8(lns[us], f"lns{us}",
                           (lambda e, u=u: e.copy(lnTa[:, 0:1024], pT[:, :])),
                           ["mixT"])
                pq = pS if us == 0 else pC
                pqk = "pS" if us == 0 else "pC0"
                def fq(e, u=u, pq=pq):
                    last = None
                    for c in range(NCH):
                        last = e.matmul(pq[:, 0:128], wqt[:, 128 * c:128 * (c + 1)],
                                        lnTa[:, 128 * c:128 * (c + 1)],
                                        start=(c == 0), stop=(c == NCH - 1))
                    return last
                S.add("pe", fq, reads=["wqt", "mixT"], writes=[pqk], cost=500)
                S.add("act", lambda e, t=t, pq=pq: e.copy(
                    QTs[:, 128 * t:128 * (t + 1)], pq[:, 0:128]),
                    reads=[pqk], writes=[f"QT{t}"], cost=350)
                klo = 512 * us
                kbk = "pA" + str(us)
                def fk(e, klo=klo):
                    last = None
                    for c in range(NCH):
                        last = e.matmul(pA[:, klo:klo + 128],
                                        lnTa[:, 128 * c:128 * (c + 1)],
                                        wkt[:, 128 * c:128 * (c + 1)],
                                        start=(c == 0), stop=(c == NCH - 1))
                    return last
                S.add("pe", fk, reads=["mixT", "wkt"], writes=[kbk], cost=500)
                S.add("act", lambda e, us=us, klo=klo: e.copy(
                    K2[us][:, :], pA[:, klo:klo + 128]), reads=[kbk],
                    writes=[f"K2{us}"], cost=350)
                for hh, pdst, pdk in ((0, pT32, "pT"), (1, pS, "pS")):
                    def fMh(e, us=us, hh=hh, pdst=pdst):
                        return e.matmul(pdst[:, 0:512], K2[us][:, :],
                                        lns[us][:, 512 * hh:512 * (hh + 1)],
                                        start=True, stop=True)
                    S.add("pe", fMh, reads=[f"K2{us}", f"lns{us}"], writes=[pdk],
                          cost=300)
                    if t == 0:
                        S.add("dve", lambda e, hh=hh, pdst=pdst: e.tensor_copy(
                            Macc[:, 512 * hh:512 * (hh + 1)], pdst[:, 0:512]),
                            reads=[pdk], writes=["Macc"], cost=700)
                    else:
                        S.add("dve", lambda e, hh=hh, pdst=pdst: e.tensor_tensor(
                            Macc[:, 512 * hh:512 * (hh + 1)],
                            Macc[:, 512 * hh:512 * (hh + 1)], pdst[:, 0:512],
                            ALU.add),
                            reads=[pdk, "Macc"], writes=["Macc"], cost=750)

        def attn_B3_group(g):
            for u in range(GRP):
                t = g * GRP + u
                us = u % 2
                jb = Jn[us]
                jk = f"Jn{us}"
                pXa, pna = ((pA, "pA"), (pB, "pB"), (pC, "pC"))[t % 3]
                for h in range(2):
                    S.add("pe", lambda e, t=t, h=h, pXa=pXa: e.matmul(
                        pXa[:, 512 * h:512 * (h + 1)], QTs[:, 128 * t:128 * (t + 1)],
                        Msb[:, 512 * h:512 * (h + 1)], start=True, stop=True),
                        reads=[f"QT{t}", "Msb"], writes=[pna + str(h)], cost=300)
                    S.add("dve", lambda e, t=t, jb=jb, h=h, pXa=pXa:
                          e.scalar_tensor_tensor(
                        jb[:, 512 * h:512 * (h + 1)],
                        x[:, 1024 * t + 512 * h:1024 * t + 512 * (h + 1)],
                        r2a[:, t:t + 1], pXa[:, 512 * h:512 * (h + 1)],
                        ALU.mult, ALU.add),
                        reads=[pna + str(h), f"x{t}", "r2a"], writes=[jk], cost=750)
                def f3(e, stp, us=us, jb=jb):
                    stp(e.bn_stats(bn6[us][:, 0:6], jb[:, 0:512]))
                    stp(e.bn_stats(bn6[us][:, 6:12], jb[:, 512:1024]))
                    stp(e.bn_aggr(mv3[us][:, :], bn6[us][:, 0:12]))
                S.add("dve", f3, reads=[jk], writes=[f"mv3{us}", f"bn6{us}"],
                      n_inc=3, cost=1800)
                def f4(e, stp, us=us):
                    stp(e.activation(s_t3[us][:, 0:1], mv3[us][:, 1:2], AF.Ln,
                                     bias=consts[:, 1:2]))
                    stp(e.activation(s_t3[us][:, 1:2], s_t3[us][:, 0:1], AF.Exp,
                                     scale=-0.5))
                S.add("act", f4, reads=[f"mv3{us}", "ones"], writes=[f"s_t3{us}"],
                      n_inc=2, cost=500)
                S.add("dve", lambda e, us=us: e.scalar_tensor_tensor(
                    s_b3[us][:, :], mv3[us][:, 0:1], -1.0, s_t3[us][:, 1:2],
                    ALU.mult, ALU.mult),
                    reads=[f"mv3{us}", f"s_t3{us}"], writes=[f"s_b3{us}"], cost=250)
                S.add("act", lambda e, us=us, jb=jb: e.activation(
                    jb[:, :], jb[:, :], AF.Identity,
                    scale=s_t3[us][:, 1:2], bias=s_b3[us][:, :]),
                    reads=[jk, f"s_t3{us}", f"s_b3{us}"], writes=[jk], cost=1100)
                S.add("pool", lambda e, jb=jb: e.tensor_tensor(jb[:, :], jb[:, :],
                                                               gmb[:, :], ALU.mult),
                      reads=[jk, "gmb"], writes=[jk], cost=2400)
                S.add("pool", lambda e, jb=jb: e.tensor_tensor(jb[:, :], jb[:, :],
                                                               btb[:, :], ALU.add),
                      reads=[jk, "btb"], writes=[jk], cost=2400)
                S.add("pool", lambda e, t=t, jb=jb: e.tensor_tensor(
                    x[:, 1024 * t:1024 * (t + 1)], x[:, 1024 * t:1024 * (t + 1)],
                    jb[:, :], ALU.add),
                    reads=[f"x{t}", jk], writes=[f"x{t}"], cost=2400)

        for s in range(n_seq):
            if n_sub >= 2:
                emit_ffn(s, second=False,
                         group_post=attn_B1_group if il_b1 else None)
                if not il_b1:
                    for g in range(NT // GRP):
                        attn_B1_group(g)
                S.add("act", lambda e: e.activation(Msb[:, :], Macc[:, :], AF.Identity,
                                                    scale=INV_SQRT_DK),
                      reads=["Macc"], writes=["Msb"], cost=1100)
            elif n_sub >= 1:
                emit_ffn(s, second=False)
            if n_sub >= 3:
                if il_b3:
                    emit_ffn(s, second=True, group_pre=attn_B3_group)
                else:
                    for g in range(NT // GRP):
                        attn_B3_group(g)
                    emit_ffn(s, second=True)
            for t in range(NT):
                S.add("qout", lambda e, t=t, s=s: e.dma_start(
                    xout[T * s + 128 * t:T * s + 128 * (t + 1), :],
                    x[:, 1024 * t:1024 * (t + 1)]), reads=[f"x{t}"],
                    cost=300, xfer=4000)
                if s < n_seq - 1:
                    S.add("qin", lambda e, t=t, s=s: e.dma_start(
                        x[:, 1024 * t:1024 * (t + 1)],
                        xin[T * (s + 1) + 128 * t:T * (s + 1) + 128 * (t + 1), :]),
                        writes=[f"x{t}"], cost=300, xfer=4000)

        dumpables = {
            "rw0": (rw[0][:, :], "rw0"), "lg0": (lg[0][:, :], "lg0"),
            "mv0": (mv[0][:, :], "mv0"), "rstdg0": (rstdg[0][:, :], "rstdg0"),
            "invng0": (invng[0][:, :], "invng0"), "biasg0": (biasg[0][:, :], "biasg0"),
            "lns0": (lns[0][:, :], "lns0"), "qtok0": (qtok[0][:, :], "qtok0"),
            "vsb": (vsb[:, :], "v15"), "attn0": (attn[0][:, :], "attn0"),
            "Psb": (Psb[0][:, :], "P0"), "PTs": (PTs[:, :], "PT"),
            "s_rcp": (s_rcp[0][:, :], "s_rcp0"), "sgb": (sgb[:, :], "sgb"),
            "simn": (simn[:, :], "simn"), "mixed": (mixed[:, :], "mixed"),
            "mixT": (mixT[:, :], "mixT"), "lnT0": (lnT[0][:, :], "lnT0"),
            "qTs": (qTs[:, :], "qT15"), "kTs": (kTs[:, :], "kT15"),
            "s_rsb": (s_rsb[:, :], "rsb"), "g_gt": (g_gt[:, :], "g_gt"),
            "g_ps": (g_ps[:, :], "g_ps"), "prod": (prod[:, :], "prod"),
            "Msb": (Msb[:, :], "Msb"), "QTs": (QTs[:, :], "QT15"),
            "Jn": (Jn[0][:, :], "Jn0"), "r2a": (r2a[:, :], "r2a"),
        }
        for i, nm in enumerate(dump):
            ap, key = dumpables[nm]
            rows = ap.shape[0]
            cols = int(np.prod(ap.shape[1:]))
            base = 4096 + 128 * i
            if ap.dtype == F32:
                S.add("qout", lambda e, ap=ap, base=base, rows=rows, cols=cols:
                      e.dma_start(xout[base:base + rows, 0:cols], ap,
                                  allow_slow_non_contiguous=True), reads=[key])
            else:
                S.add("qw", lambda e, ap=ap, base=base, rows=rows, cols=cols:
                      e.dma_start(xout[base:base + rows, 0:cols], ap,
                                  allow_slow_non_contiguous=True), reads=[key])

        S.schedule()
        streams = S.gen_streams()
        n_stores = (S.cnt["qout0"], S.cnt["qout1"])

        block = st.enter_context(nc.Block())

        def mk(stream, final=False):
            strict = stream in Sched.STRICT

            def run(eng):
                for waits, fn, semh, amt, n_inc, basec, stepped in streams[stream]:
                    for sh, v in waits:
                        eng.wait_ge(sh, v)
                    if strict:
                        eng.wait_ge(semh, basec)
                    if amt == 16 and basec > 0:
                        eng.wait_ge(semh, 16 * basec)
                    if not stepped:
                        inst = fn(eng)
                        inst.then_inc(semh, amt)
                    else:
                        stp = _Step(eng, semh, basec)
                        fn(eng, stp)
                        assert stp.c == basec + n_inc, (stream, stp.c, basec, n_inc)
                if final:
                    eng.wait_ge(sems["qout0"], 16 * n_stores[0])
                    eng.wait_ge(sems["qout1"], 16 * n_stores[1])
            return run

        block.tensor(mk("tensor"))
        block.scalar(mk("scalar"))
        block.vector(mk("vector"))
        block.gpsimd(mk("gpsimd"))
        block.sync(mk("sync", final=True))

    return nc


def _host_inputs(inputs):
    J = np.ascontiguousarray(np.asarray(inputs["L0_attn_J"], dtype=np.float32))
    WQ = np.asarray(inputs["L0_attn_W_Q"], dtype=np.float32)
    WK = np.asarray(inputs["L0_attn_W_K"], dtype=np.float32)
    gamma = np.asarray(inputs["L0_attn_gamma"], dtype=np.float32).reshape(1, D)
    beta = np.asarray(inputs["L0_attn_beta"], dtype=np.float32).reshape(1, D)
    sim = np.asarray(inputs["L0_ffn_sim_matrix"], dtype=np.float32)
    gates = np.asarray(inputs["L0_ffn_gates"], dtype=np.float32).reshape(1, E)
    qp = np.asarray(inputs["L0_ffn_q_proj"], dtype=np.float32)
    kp = np.asarray(inputs["L0_ffn_k_proj"], dtype=np.float32)
    vp = np.asarray(inputs["L0_ffn_v_proj"], dtype=np.float32)
    op = np.asarray(inputs["L0_ffn_o_proj"], dtype=np.float32)

    common = {
        "wq": np.ascontiguousarray(qp.transpose(1, 0, 2).reshape(D, E * HD)),
        "wk": np.ascontiguousarray(kp.transpose(1, 0, 2).reshape(D, E * HD)),
        "wv": np.ascontiguousarray(vp.transpose(1, 0, 2).reshape(D, E * HD)),
        "wo": np.ascontiguousarray(op.reshape(E * HD, D)),
        "wqt": np.ascontiguousarray(WQ.T),
        "wkt": np.ascontiguousarray(WK.T),
        "simr": np.ascontiguousarray(sim.reshape(NCH, 128, E).transpose(1, 0, 2)
                                     .reshape(128, NCH * E)),
        "gates": np.ascontiguousarray(gates),
        "gamma": np.ascontiguousarray(gamma),
        "beta": np.ascontiguousarray(beta),
        "ident": np.eye(128, dtype=BF),
        "trimask": np.where(np.arange(128)[None, :] <= np.arange(128)[:, None],
                            np.float32(0.0), np.float32(-BIGV)).astype(BF),
        "consts": np.broadcast_to(np.array([1.0, LN_EPS, 1.0e-18, 0.0],
                                           dtype=np.float32), (128, 4)).copy(),
    }
    maps = []
    for c in range(NCORES):
        m = dict(common)
        m["xin"] = np.ascontiguousarray(J[SPC * c:SPC * (c + 1)].reshape(SPC * T, D))
        maps.append(m)
    return maps


def kernel(**inputs):
    in_maps = _host_inputs(inputs)
    nc = build_program()
    res = run_bass_kernel_spmd(nc, in_maps, core_ids=list(range(NCORES)))
    out = np.concatenate(
        [np.asarray(r["xout"]).reshape(SPC, T, D) for r in res.results], axis=0)
    return np.ascontiguousarray(out.astype(np.float32))


# version 20 (final)
# speedup vs baseline: 1.0819x; 1.0819x over previous
import numpy as np
import ml_dtypes
from contextlib import ExitStack

import concourse.bass as bass
import concourse.mybir as mybir
from concourse.bass_utils import run_bass_kernel_spmd

F32 = mybir.dt.float32
BF16 = mybir.dt.bfloat16
AF = mybir.ActivationFunctionType
ALU = mybir.AluOpType
AX = mybir.AxisListType
BF = ml_dtypes.bfloat16

B, T, D, E, HD, DK = 32, 2048, 1024, 16, 64, 128
NCORES = 8
SPC = B // NCORES
NT = T // 128
NCH = D // 128
GRP = 4
BIGV = 1.0e30
LN_EPS = 1.0e-5
INV_SQRT_DK = float(1.0 / np.sqrt(DK))


class _Step:

    def __init__(self, eng, sem, base):
        self.eng, self.sem, self.c = eng, sem, base

    def __call__(self, r):
        self.c += 1
        r.then_inc(self.sem, 1)
        self.eng.wait_ge(self.sem, self.c)
        return r


HOP_NS = 150.0


class Sched:
    STREAM = {"pe": "tensor", "act": "scalar", "dve": "vector", "pool": "gpsimd",
              "qin0": "sync", "qin1": "sync", "qout0": "sync", "qout1": "sync",
              "qw0": "gpsimd", "qw1": "gpsimd"}
    DMA = {"qin0", "qin1", "qout0", "qout1", "qw0", "qw1"}
    QSPLIT = {"qin": 2, "qout": 2, "qw": 2}
    STRICT = {"vector", "scalar", "gpsimd"}
    STREAMS = ("tensor", "scalar", "vector", "gpsimd", "sync")

    def __init__(self, sems):
        self.sems = sems
        self.rec = []
        self.writer = {}
        self.readers = {}
        self.qrr = {}

    def add(self, peng, fn, reads=(), writes=(), n_inc=1, stepped=None,
            cost=500.0, xfer=0.0):
        if peng in self.QSPLIT:
            k = self.qrr.get(peng, 0)
            self.qrr[peng] = k + 1
            peng = f"{peng}{k % self.QSPLIT[peng]}"
        deps = set()
        for b in reads:
            w = self.writer.get(b)
            if w is not None:
                deps.add(w)
        for b in writes:
            w = self.writer.get(b)
            if w is not None:
                deps.add(w)
            deps.update(self.readers.get(b, ()))
        i = len(self.rec)
        if peng == "dve":
            cost = float(cost) + 250.0 * max(1, n_inc)
        elif peng == "act":
            cost = float(cost) + 150.0 * max(1, n_inc)
        self.rec.append(dict(
            i=i, peng=peng, fn=fn, n_inc=n_inc,
            stepped=(n_inc > 1 if stepped is None else stepped),
            cost=float(cost), xfer=float(xfer), deps=deps,
            reads=tuple(reads), writes=tuple(writes)))
        for b in writes:
            self.writer[b] = i
            self.readers[b] = set()
        for b in reads:
            self.readers.setdefault(b, set()).add(i)
        return i

    def schedule(self, eps=1000.0):
        n = len(self.rec)
        children = [[] for _ in range(n)]
        indeg = [0] * n
        for r in self.rec:
            for d in r["deps"]:
                children[d].append(r["i"])
                indeg[r["i"]] += 1
        rank = [0.0] * n
        for i in range(n - 1, -1, -1):
            r = self.rec[i]
            mx = 0.0
            for c in children[i]:
                if rank[c] > mx:
                    mx = rank[c]
            rank[i] = mx + r["cost"] + r["xfer"]
        finish = [0.0] * n
        rt = [0.0] * n
        stream_free = {st: 0.0 for st in self.STREAMS}
        readyset = set(i for i in range(n) if indeg[i] == 0)
        order = []
        while len(order) < n:
            est_min = None
            ests = {}
            for i in readyset:
                r = self.rec[i]
                est = stream_free[self.STREAM[r["peng"]]]
                if rt[i] > est:
                    est = rt[i]
                ests[i] = est
                if est_min is None or est < est_min:
                    est_min = est
            best = None
            best_key = None
            for i, est in ests.items():
                if est <= est_min + eps:
                    key = (-rank[i], est, i)
                    if best_key is None or key < best_key:
                        best_key = key
                        best = i
            i = best
            readyset.discard(i)
            r = self.rec[i]
            st = self.STREAM[r["peng"]]
            start = ests[i]
            stream_free[st] = start + r["cost"]
            finish[i] = start + r["cost"] + r["xfer"]
            order.append(i)
            for c in children[i]:
                hop = HOP_NS if self.STREAM[self.rec[c]["peng"]] != st else 0.0
                v = finish[i] + hop
                if v > rt[c]:
                    rt[c] = v
                indeg[c] -= 1
                if indeg[c] == 0:
                    readyset.add(c)
        self.order = order
        self.makespan = max(finish) if n else 0.0
        return order

    def gen_streams(self):
        cnt = {p: 0 for p in self.STREAM}
        writer = {}
        readers = {}
        mark = {}
        out = {st: [] for st in self.STREAMS}
        for i in self.order:
            r = self.rec[i]
            peng = r["peng"]
            stream = self.STREAM[peng]
            deps = {}
            def dep(p, c):
                if c > deps.get(p, 0):
                    deps[p] = c
            for b in r["reads"]:
                w = writer.get(b)
                if w is not None:
                    dep(*w)
            for b in r["writes"]:
                w = writer.get(b)
                if w is not None:
                    dep(*w)
                for p, c in readers.get(b, {}).items():
                    dep(p, c)
            waits = []
            for p, c in sorted(deps.items()):
                if p == peng:
                    continue
                assert c <= cnt[p], f"forward dep {p}:{c} > {cnt[p]}"
                if p in self.DMA:
                    c = cnt[p]
                key = (stream, p)
                if mark.get(key, 0) >= c:
                    continue
                mark[key] = c
                waits.append((self.sems[p], 16 * c if p in self.DMA else c))
            base = cnt[peng]
            cnt[peng] += r["n_inc"]
            cn = cnt[peng]
            out[stream].append((waits, r["fn"], self.sems[peng],
                                16 if peng in self.DMA else 1, r["n_inc"], base,
                                r["stepped"]))
            for b in r["writes"]:
                writer[b] = (peng, cn)
                readers[b] = {}
            for b in r["reads"]:
                readers.setdefault(b, {})[peng] = cn
        self.cnt = cnt
        return out


def build_program(n_seq=SPC, n_sub=3, dump=(), il_b1=False, il_b3=False):
    nc = bass.Bass("TRN2", target_bir_lowering=False)

    xin = nc.dram_tensor("xin", [SPC * T, D], F32, kind="ExternalInput")
    xout = nc.dram_tensor("xout", [SPC * T, D], F32, kind="ExternalOutput")
    wq_d = nc.dram_tensor("wq", [D, E * HD], F32, kind="ExternalInput")
    wk_d = nc.dram_tensor("wk", [D, E * HD], F32, kind="ExternalInput")
    wv_d = nc.dram_tensor("wv", [D, E * HD], F32, kind="ExternalInput")
    wo_d = nc.dram_tensor("wo", [E * HD, D], F32, kind="ExternalInput")
    wqt_d = nc.dram_tensor("wqt", [D, DK], F32, kind="ExternalInput")
    wkt_d = nc.dram_tensor("wkt", [D, DK], F32, kind="ExternalInput")
    simr_d = nc.dram_tensor("simr", [128, 128], F32, kind="ExternalInput")
    gates_d = nc.dram_tensor("gates", [1, E], F32, kind="ExternalInput")
    gamma_d = nc.dram_tensor("gamma", [1, D], F32, kind="ExternalInput")
    beta_d = nc.dram_tensor("beta", [1, D], F32, kind="ExternalInput")
    ident_d = nc.dram_tensor("ident", [128, 128], BF16, kind="ExternalInput")
    tri_d = nc.dram_tensor("trimask", [128, 128], BF16, kind="ExternalInput")
    ones_d = nc.dram_tensor("consts", [128, 4], F32, kind="ExternalInput")
    dscr = nc.dram_tensor("dscr", [1, 64], F32)

    with ExitStack() as st:
        sb = lambda n, sh, dt: st.enter_context(nc.sbuf_tensor(n, sh, dt))
        psm = lambda n, sh, dt: st.enter_context(nc.psum_tensor(n, sh, dt))
        sem = lambda n: st.enter_context(nc.semaphore(n))

        wq = sb("wq_s", [128, NCH * 1024], BF16)
        wk = sb("wk_s", [128, NCH * 1024], BF16)
        wv = sb("wv_s", [128, NCH * 1024], BF16)
        wo = sb("wo_s", [128, NCH * 1024], BF16)
        wqt = sb("wqt_s", [128, NCH * 128], BF16)
        wkt = sb("wkt_s", [128, NCH * 128], BF16)
        simn = sb("simn_s", [128, NCH * 16], BF16)

        sgb = sb("sgb_s", [128, E], F32)
        gmb = sb("gmb_s", [128, D], BF16)
        btb = sb("btb_s", [128, D], BF16)
        ident = sb("ident_s", [128, 128], BF16)
        tri = sb("tri_s", [128, 128], BF16)
        consts = sb("consts_s", [128, 4], F32)
        x = sb("x_s", [128, NT * 1024], F32)
        lnT = [sb(f"lnT{i}", [128, GRP * 1024], BF16) for i in range(2)]
        lns = [sb(f"lns{i}", [128, 1024], BF16) for i in range(2)]
        prod = sb("prod_s", [128, 1024], F32)
        s_cn = prod[0:1, 256:512]
        s_rs = prod[0:1, 512:544]
        Macc = sb("Macc_s", [128, 1024], F32)
        qtok = [sb(f"qtok{i}", [128, 128], BF16) for i in range(2)]
        vsb = sb("v_s", [128, NT * 64], BF16)
        qTs = sb("qT_s", [64, T], BF16)
        kTs = sb("kT_s", [64, T], BF16)
        Psb_ = sb("P_s0", [128, T], BF16)
        Psb = [Psb_, Psb_]
        PTs = sb("PT_s", [128, T], BF16)
        attn = [sb(f"attn{i}", [128, 64], BF16) for i in range(2)]
        mixed = sb("mixed_s", [128, 1024], BF16)
        mixT = sb("mixT_s", [128, 1024], BF16)
        lnTa = mixT
        lg = [sb(f"lg{i}", [128, 16 * GRP], F32) for i in range(4)]
        rw = [sb(f"rw{i}", [128, 16 * GRP], F32) for i in range(3)]
        mv = [sb(f"mv{i}", [128, 2 * GRP], F32) for i in range(4)]
        rstdg = [sb(f"rstdg{i}", [128, GRP], F32) for i in range(4)]
        biasg = [sb(f"biasg{i}", [128, GRP], F32) for i in range(4)]
        invng = [sb(f"invng{i}", [128, GRP], F32) for i in range(4)]
        stg = [sb(f"stg{i}", [128, 2 * GRP], F32) for i in range(4)]
        bn6 = [sb(f"bn6{i}", [128, 12], F32) for i in range(4)]
        g_gt = sb("g_gt", [128, 16 * GRP], F32)
        g_ps = sb("g_ps", [128, 16 * GRP], F32)
        g_i1 = sb("g_i1", [128, 16 * GRP], F32)
        g_t2 = sb("g_t2", [128, 16 * GRP], F32)
        g_m = sb("g_m", [128, 16 * GRP], F32)
        s_np = sb("s_np", [128, GRP], F32)
        s_ac = sb("s_ac", [128, GRP], F32)
        s_in = sb("s_in", [128, GRP], F32)
        s_m1 = sb("s_m1", [128, GRP], F32)
        s_m2 = sb("s_m2", [128, GRP], F32)
        QTs = sb("QT_s", [128, T], BF16)
        K2 = [sb(f"K2{i}", [128, 128], BF16) for i in range(2)]
        Msb = sb("M_s", [128, 1024], BF16)
        Jn = [sb(f"Jn_s{i}", [128, 1024], F32) for i in range(2)]
        r2a = sb("r2a_s", [128, NT], F32)
        b2a = sb("b2a_s", [128, NT], F32)
        mv3 = [sb(f"mv3_s{i}", [128, 2], F32) for i in range(2)]
        s_t3 = [sb(f"s_t3{i}", [128, 2], F32) for i in range(2)]
        s_b3 = [sb(f"s_b3{i}", [128, 1], F32) for i in range(2)]
        s_nmA = sb("s_nmA", [128, 1], F32)
        s_nmB = sb("s_nmB", [128, 1], F32)
        s_nm = sb("s_nm", [128, 1], F32)
        s_nm8 = [sb(f"s_nm8_{i}", [128, 1], F32) for i in range(2)]
        s_suA = sb("s_suA", [128, 1], F32)
        s_suB = sb("s_suB", [128, 1], F32)
        s_sum = sb("s_sum", [128, 1], F32)
        s_rcp = [sb(f"s_rcp{i}", [128, 1], F32) for i in range(2)]
        s_nmp = sb("s_nmp", [128, 4], F32)
        s_sup = [sb(f"s_sup{i}", [128, 4], F32) for i in range(2)]
        s_sg = sb("s_sg", [1, 64], F32)
        s_rsb = sb("s_rsb", [128, 16], F32)

        pA = psm("pA", [128, 1024], F32)
        pB = psm("pB", [128, 1024], F32)
        pC = psm("pC", [128, 1024], F32)
        pT = psm("pT", [128, 1024], BF16)
        pS = psm("pS", [128, 512], F32)

        sems = {p: sem(f"sem_{p}") for p in Sched.STREAM}
        S = Sched(sems)

        def bankkeys(name, lo, hi):
            ks = []
            if lo < 512:
                ks.append(name + "0")
            if hi > 512:
                ks.append(name + "1")
            return ks

        S.add("qin", lambda e: e.dma_start(ident[:, :], ident_d.ap()), writes=["ident"])
        S.add("qin", lambda e: e.dma_start(tri[:, :], tri_d.ap()), writes=["tri"])
        S.add("qin", lambda e: e.dma_start(consts[:, :], ones_d.ap()), writes=["ones"])
        S.add("qin", lambda e: e.dma_start(s_sg[0:1, 0:16], gates_d.ap()), writes=["sg_raw"])
        simr = prod[:, 128:256]
        S.add("qin", lambda e: e.dma_start(simr, simr_d.ap()), writes=["prod0"])
        for t in range(NT):
            S.add("qin", lambda e, t=t: e.dma_start(
                x[:, 1024 * t:1024 * (t + 1)], xin[128 * t:128 * (t + 1), :]),
                writes=[f"x{t}"], cost=300, xfer=4000)

        for w_s, w_d, key in ((wq, wq_d, "wq"), (wk, wk_d, "wk"), (wv, wv_d, "wv"),
                              (wo, wo_d, "wo"), (wqt, wqt_d, "wqt"), (wkt, wkt_d, "wkt")):
            S.add("qw", lambda e, w_s=w_s, w_d=w_d: e.dma_start(
                w_s[:, :].rearrange("p (c n) -> p c n", c=NCH),
                w_d.ap().rearrange("(c p) n -> p c n", p=128)), writes=[key],
                cost=1200, xfer=16000)
        S.add("qw", lambda e: e.dma_start(
            gmb[:, :], bass.AP(gamma_d, 0, [[0, 128], [1, D]])), writes=["gmb"],
            cost=1000, xfer=4000)
        S.add("qw", lambda e: e.dma_start(
            btb[:, :], bass.AP(beta_d, 0, [[0, 128], [1, D]])), writes=["btb"],
            cost=1000, xfer=4000)

        S.add("act", lambda e: e.activation(s_sg[0:1, 16:32], s_sg[0:1, 0:16], AF.Exp,
                                            scale=-1.0), reads=["sg_raw"], writes=["sg_e"])
        S.add("dve", lambda e: e.tensor_scalar(s_sg[0:1, 32:48], s_sg[0:1, 16:32],
                                               1.0, None, ALU.add),
              reads=["sg_e"], writes=["sg_v"])
        S.add("dve", lambda e: e.reciprocal(s_sg[0:1, 48:64], s_sg[0:1, 32:48]),
              reads=["sg_v"], writes=["sg_r"])
        S.add("qout", lambda e: e.dma_start(dscr[0:1, 0:16], s_sg[0:1, 48:64]),
              reads=["sg_r"], writes=["dscr"])
        S.add("qin", lambda e: e.dma_start(sgb[:, :], bass.AP(dscr, 0, [[0, 128], [1, 16]])),
              reads=["dscr"], writes=["sgb"])

        S.add("dve", lambda e: e.tensor_tensor(prod[:, 0:128], simr, simr,
                                               ALU.mult), reads=["prod0"], writes=["prod0"])
        S.add("pe", lambda e: e.matmul(pS[0:1, 0:128], consts[:, 0:1], prod[:, 0:128],
                                       start=True, stop=True),
              reads=["ones", "prod0"], writes=["pS"])
        def cn_fn(e, stp):
            stp(e.tensor_copy(prod[0:1, 256:384], pS[0:1, 0:128]))
            stp(e.tensor_tensor(prod[0:1, 384:448], s_cn[0:1, 0:64], s_cn[0:1, 64:128], ALU.add))
            stp(e.tensor_tensor(prod[0:1, 448:480], s_cn[0:1, 128:160], s_cn[0:1, 160:192], ALU.add))
            stp(e.tensor_tensor(prod[0:1, 480:496], s_cn[0:1, 192:208], s_cn[0:1, 208:224], ALU.add))
            stp(e.tensor_scalar(prod[0:1, 496:512], prod[0:1, 480:496], 1.0e-18, None,
                                ALU.max))
        S.add("dve", cn_fn, reads=["pS"], writes=["prod0"], n_inc=5)
        def rs_fn(e, stp):
            stp(e.activation(prod[0:1, 512:528], prod[0:1, 496:512], AF.Ln))
            stp(e.activation(prod[0:1, 528:544], prod[0:1, 512:528], AF.Exp, scale=-0.5))
        S.add("act", rs_fn, reads=["prod0"], writes=["prod1"], n_inc=2)
        S.add("qout", lambda e: e.dma_start(dscr[0:1, 16:32], prod[0:1, 528:544]),
              reads=["prod1"], writes=["dscr2"])
        S.add("qin", lambda e: e.dma_start(s_rsb[:, :],
                                           bass.AP(dscr, 16, [[0, 128], [1, 16]])),
              reads=["dscr2"], writes=["rsb"])
        S.add("dve", lambda e: e.tensor_tensor(
            simn[:, :].rearrange("p (c e) -> p c e", c=NCH),
            simr.rearrange("p (c e) -> p c e", c=NCH),
            s_rsb[:, :].unsqueeze(1).broadcast_to([128, NCH, 16]), ALU.mult),
            reads=["prod0", "rsb"], writes=["simn"])

        def ln_group_stats(g, sl, with_invn, rstd_ap, bias_ap, wkeys):
            for u in range(GRP):
                t = g * GRP + u
                def f(e, stp, t=t, u=u, sl=sl):
                    stp(e.bn_stats(bn6[sl][:, 0:6], x[:, 1024 * t:1024 * t + 512]))
                    stp(e.bn_stats(bn6[sl][:, 6:12], x[:, 1024 * t + 512:1024 * t + 1024]))
                    stp(e.bn_aggr(mv[sl][:, 2 * u:2 * u + 2], bn6[sl][:, 0:12]))
                S.add("dve", f, reads=[f"x{t}"], writes=[f"mv{sl}", f"bn6{sl}"],
                      n_inc=3, cost=1800)
            varv = mv[sl][:, 1:2 * GRP:2]
            def fb(e, stp, sl=sl, varv=varv, rstd_ap=rstd_ap, with_invn=with_invn):
                stp(e.activation(stg[sl][:, 0:GRP], varv, AF.Ln, bias=consts[:, 1:2]))
                stp(e.activation(rstd_ap, stg[sl][:, 0:GRP], AF.Exp, scale=-0.5))
                if with_invn:
                    stp(e.activation(stg[sl][:, GRP:2 * GRP], varv, AF.Ln, scale=float(D),
                                     bias=consts[:, 2:3]))
                    stp(e.activation(invng[sl][:, :], stg[sl][:, GRP:2 * GRP], AF.Exp,
                                     scale=-0.5))
            S.add("act", fb, reads=[f"mv{sl}", "ones"],
                  writes=[wkeys[0], f"stg{sl}"] + ([f"invng{sl}"] if with_invn else []),
                  n_inc=4 if with_invn else 2, cost=1000)
            S.add("dve", lambda e, sl=sl, rstd_ap=rstd_ap, bias_ap=bias_ap:
                  e.scalar_tensor_tensor(bias_ap, mv[sl][:, 0:2 * GRP:2], -1.0, rstd_ap,
                                         ALU.mult, ALU.mult),
                  reads=[f"mv{sl}", wkeys[0]], writes=[wkeys[1]], cost=250)

        def ln_apply(t, u, scale_ap, bias_ap, rkeys):
            us = u % 2
            S.add("act", lambda e: e.activation(
                lns[us][:, :], x[:, 1024 * t:1024 * (t + 1)], AF.Identity,
                scale=scale_ap, bias=bias_ap),
                reads=[f"x{t}"] + rkeys, writes=[f"lns{us}"], cost=1100)

        def transpose8(src, src_key, dst_evac, dst_keys):
            def f(e):
                last = None
                for c in range(8):
                    last = e.transpose(pT[:, 128 * c:128 * (c + 1)],
                                       src[:, 128 * c:128 * (c + 1)], ident[:, :])
                return last
            S.add("pe", f, reads=[src_key, "ident"], writes=["pT"], cost=600)
            S.add("act", dst_evac, reads=["pT"], writes=dst_keys, cost=1100)

        pSb = pS[:, :].bitcast(BF16)

        def emit_ffn(s, second, group_pre=None, group_post=None):
            for g in range(NT // GRP):
                sl = g % 4
                rsl = g % 3
                if group_pre is not None:
                    group_pre(g)
                ln_group_stats(g, sl, True, rstdg[sl][:, :], biasg[sl][:, :],
                               [f"rstdg{sl}", f"biasg{sl}"])
                for u in range(GRP):
                    t = g * GRP + u
                    us = u % 2
                    ln_apply(t, u, rstdg[sl][:, u:u + 1], biasg[sl][:, u:u + 1],
                             [f"rstdg{sl}", f"biasg{sl}"])
                    transpose8(lns[us], f"lns{us}",
                               (lambda e, sl=sl, u=u: e.copy(
                                   lnT[sl % 2][:, 1024 * u:1024 * (u + 1)], pT[:, :])),
                               [f"lnT{sl % 2}"])
                    def fg(e, sl=sl, u=u):
                        last = None
                        for c in range(NCH):
                            last = e.matmul(pS[:, 0:16],
                                            lnT[sl % 2][:, 1024 * u + 128 * c:1024 * u + 128 * (c + 1)],
                                            simn[:, 16 * c:16 * (c + 1)],
                                            start=(c == 0), stop=(c == NCH - 1))
                        return last
                    S.add("pe", fg, reads=[f"lnT{sl % 2}", "simn"], writes=["pS"], cost=600)
                    S.add("dve", lambda e, sl=sl, u=u: e.scalar_tensor_tensor(
                        lg[sl][:, 16 * u:16 * (u + 1)], pS[:, 0:16],
                        invng[sl][:, u:u + 1], sgb[:, :], ALU.mult, ALU.subtract),
                        reads=["pS", f"invng{sl}", "sgb"], writes=[f"lg{sl}"], cost=250)
                def epi(e, stp, sl=sl):
                    lgv = lg[sl][:, :]
                    lg3 = lgv.rearrange("p (s e) -> p s e", e=16)
                    def bc(a):
                        return a.unsqueeze(2).broadcast_to([128, GRP, 16])
                    def r3(a):
                        return a.rearrange("p (s e) -> p s e", e=16)
                    stp(e.tensor_scalar(g_gt[:, :], lgv, 0.0, None, ALU.max))
                    stp(e.tensor_scalar(g_ps[:, :], lgv, 0.0, None, ALU.is_gt))
                    stp(e.tensor_reduce(s_np[:, :], r3(g_ps[:, :]), AX.X, ALU.add))
                    stp(e.tensor_scalar(s_ac[:, :], s_np[:, :], 0.0, None, ALU.is_gt))
                    stp(e.tensor_scalar(s_in[:, :], s_ac[:, :], -1.0, 1.0, ALU.mult,
                                        ALU.add))
                    stp(e.tensor_reduce(s_m1[:, :], lg3, AX.X, ALU.max))
                    stp(e.tensor_tensor(r3(g_i1[:, :]), lg3, bc(s_m1[:, :]), ALU.is_ge))
                    stp(e.scalar_tensor_tensor(g_m[:, :], g_i1[:, :], -BIGV, lgv,
                                               ALU.mult, ALU.add))
                    stp(e.tensor_reduce(s_m2[:, :], r3(g_m[:, :]), AX.X, ALU.max))
                    stp(e.tensor_tensor(r3(g_t2[:, :]), lg3, bc(s_m2[:, :]), ALU.is_ge))
                    stp(e.tensor_tensor(r3(g_i1[:, :]), r3(g_ps[:, :]), bc(s_ac[:, :]),
                                        ALU.mult))
                    stp(e.tensor_tensor(r3(g_m[:, :]), r3(g_t2[:, :]), bc(s_in[:, :]),
                                        ALU.mult))
                    stp(e.tensor_tensor(g_ps[:, :], g_i1[:, :], g_m[:, :], ALU.add))
                    stp(e.tensor_scalar(g_m[:, :], g_ps[:, :], -1.0, BIGV, ALU.add,
                                        ALU.mult))
                    stp(e.tensor_tensor(g_t2[:, :], g_m[:, :], g_gt[:, :], ALU.add))
                    stp(e.tensor_reduce(s_m1[:, :], r3(g_t2[:, :]), AX.X, ALU.max))
                    stp(e.tensor_tensor(r3(g_ps[:, :]), r3(g_t2[:, :]), bc(s_m1[:, :]),
                                        ALU.subtract))
                S.add("dve", epi, reads=[f"lg{sl}", "sgb"],
                      writes=["g_gt", "g_ps", "g_i1", "g_t2", "g_m",
                              "s_np", "s_ac", "s_in", "s_m1", "s_m2"], n_inc=17,
                      cost=3500)
                S.add("act", lambda e: e.activation(g_i1[:, :], g_ps[:, :], AF.Exp),
                      reads=["g_ps"], writes=["g_i1"], cost=300)
                def epi2(e, stp, rsl=rsl):
                    def bc(a):
                        return a.unsqueeze(2).broadcast_to([128, GRP, 16])
                    def r3(a):
                        return a.rearrange("p (s e) -> p s e", e=16)
                    stp(e.tensor_reduce(s_m2[:, :], r3(g_i1[:, :]), AX.X, ALU.add))
                    stp(e.reciprocal(s_np[:, :], s_m2[:, :]))
                    stp(e.tensor_tensor(r3(rw[rsl][:, :]), r3(g_i1[:, :]),
                                        bc(s_np[:, :]), ALU.mult))
                S.add("dve", epi2, reads=["g_i1"],
                      writes=[f"rw{rsl}", "s_m2", "s_np"], n_inc=3, cost=700)

                for u in range(GRP):
                    t = g * GRP + u
                    us = u % 2
                    for nm, pX, pk, w_s, wkey in (("q", pA, "pA", wq, "wq"),
                                                  ("k", pB, "pB", wk, "wk"),
                                                  ("v", pC, "pC", wv, "wv")):
                        for h in range(2):
                            def fp(e, sl=sl, u=u, pX=pX, w_s=w_s, h=h):
                                last = None
                                for c in range(NCH):
                                    last = e.matmul(
                                        pX[:, 512 * h:512 * (h + 1)],
                                        lnT[sl % 2][:, 1024 * u + 128 * c:1024 * u + 128 * (c + 1)],
                                        w_s[:, 1024 * c + 512 * h:1024 * c + 512 * (h + 1)],
                                        start=(c == 0), stop=(c == NCH - 1))
                                return last
                            S.add("pe", fp, reads=[f"lnT{sl % 2}", wkey],
                                  writes=[pk + str(h)], cost=1800)
                        for h in range(2):
                            S.add("dve", lambda e, pX=pX, rsl=rsl, u=u, h=h:
                                  e.tensor_tensor(
                                prod[:, 512 * h:512 * (h + 1)]
                                .rearrange("p (ex d) -> p ex d", ex=8),
                                pX[:, 512 * h:512 * (h + 1)]
                                .rearrange("p (ex d) -> p ex d", ex=8),
                                rw[rsl][:, 16 * u + 8 * h:16 * u + 8 * (h + 1)]
                                .unsqueeze(2).broadcast_to([128, 8, 64]), ALU.mult),
                                reads=[pk + str(h), f"rw{rsl}"],
                                writes=["prod" + str(h)], cost=750)
                        if nm == "q":
                            dst, dkey = qtok[us][:, 0:64], f"qtok{us}"
                        elif nm == "k":
                            dst, dkey = qtok[us][:, 64:128], f"qtok{us}"
                        else:
                            dst, dkey = vsb[:, 64 * t:64 * (t + 1)], f"v{t}"
                        def fred(e, dst=dst):
                            with nc.allow_low_precision(reason="fp32 red bf16 out"):
                                return e.tensor_reduce(
                                    dst, prod[:, :].rearrange("p (ex d) -> p ex d", ex=16)
                                    .transpose([0, 2, 1]), AX.X, ALU.add)
                        S.add("dve", fred, reads=["prod0", "prod1"], writes=[dkey],
                              cost=1300)

                    def ft(e, us=us):
                        e.transpose(pSb[0:64, 0:128], qtok[us][:, 0:64], ident[:, :])
                        return e.transpose(pSb[0:64, 128:256], qtok[us][:, 64:128],
                                           ident[:, :])
                    S.add("pe", ft, reads=[f"qtok{us}", "ident"], writes=["pS"], cost=300)
                    def fte(e, stp, t=t):
                        stp(e.copy(qTs[:, 128 * t:128 * (t + 1)], pSb[0:64, 0:128]))
                        stp(e.copy(kTs[:, 128 * t:128 * (t + 1)], pSb[0:64, 128:256]))
                    S.add("act", fte, reads=["pS"], writes=[f"qT{t}", f"kT{t}"],
                          n_inc=2, cost=700)

                    L = 128 * (t + 1)
                    npieces = (L + 511) // 512
                    for j in range(npieces):
                        n = min(512, L - 512 * j)
                        pX = pA if j < 2 else pB
                        lo = 512 * (j % 2)
                        pkname = ("pA" if j < 2 else "pB") + str(j % 2)
                        kkeys = [f"kT{kk}" for kk in
                                 range(4 * j, min(4 * j + (n + 127) // 128, t + 1))]
                        is_diag = (j == npieces - 1)
                        def fs(e, pX=pX, lo=lo, n=n, j=j, t=t, is_diag=is_diag):
                            last = e.matmul(pX[:, lo:lo + n], qTs[:, 128 * t:128 * (t + 1)],
                                            kTs[:, 512 * j:512 * j + n],
                                            start=True, stop=not is_diag)
                            if is_diag:
                                moff = lo + (128 * t - 512 * j)
                                last = e.matmul(pX[:, moff:moff + 128], ident[:, :],
                                                tri[:, :], start=False, stop=True)
                            return last
                        S.add("pe", fs, reads=[f"qT{t}"] + kkeys
                              + (["ident", "tri"] if is_diag else []), writes=[pkname],
                              cost=100 + n // 2)
                    La = min(L, 1024)
                    Lb = L - La
                    tp = t % 2
                    def fm(e, stp, La=La, Lb=Lb, tp=tp):
                        if Lb > 0:
                            stp(e.tensor_reduce(s_nmp[:, 0:1], pA[:, 0:La], AX.X, ALU.max,
                                                negate=True))
                            stp(e.tensor_reduce(s_nmp[:, 1:2], pB[:, 0:Lb], AX.X, ALU.max,
                                                negate=True))
                            stp(e.tensor_tensor(s_nm[:, :], s_nmp[:, 0:1], s_nmp[:, 1:2],
                                                ALU.min))
                        else:
                            stp(e.tensor_reduce(s_nm[:, :], pA[:, 0:La], AX.X, ALU.max,
                                                negate=True))
                        stp(e.tensor_scalar(s_nm8[tp][:, :], s_nm[:, :], 0.125, None,
                                            ALU.mult))
                    S.add("dve", fm, reads=bankkeys("pA", 0, La) + bankkeys("pB", 0, Lb),
                          writes=["s_nmp", "s_nm", f"s_nm8_{tp}"],
                          n_inc=4 if Lb > 0 else 2, stepped=True, cost=500 + L)
                    S.add("act", lambda e, La=La, tp=tp: e.activation(
                        Psb[tp][:, 0:La], pA[:, 0:La], AF.Exp, scale=0.125,
                        bias=s_nm8[tp][:, :], accum_out=s_sup[tp][:, 0:1]),
                        reads=bankkeys("pA", 0, La) + [f"s_nm8_{tp}"],
                        writes=["P", f"s_sup{tp}"], cost=300 + La)
                    if Lb > 0:
                        S.add("act", lambda e, Lb=Lb, tp=tp: e.activation(
                            Psb[tp][:, 1024:1024 + Lb], pB[:, 0:Lb], AF.Exp, scale=0.125,
                            bias=s_nm8[tp][:, :], accum_out=s_sup[tp][:, 1:2]),
                            reads=bankkeys("pB", 0, Lb) + [f"s_nm8_{tp}"],
                            writes=["P", f"s_sup{tp}"], cost=300 + Lb)
                    def fr(e, stp, Lb=Lb, tp=tp):
                        if Lb > 0:
                            stp(e.tensor_reduce(s_sum[:, :], s_sup[tp][:, 0:2], AX.X,
                                                ALU.add))
                            stp(e.reciprocal(s_rcp[tp][:, :], s_sum[:, :]))
                        else:
                            stp(e.reciprocal(s_rcp[tp][:, :], s_sup[tp][:, 0:1]))
                    S.add("dve", fr, reads=[f"s_sup{tp}"],
                          writes=["s_sum", f"s_rcp{tp}"],
                          n_inc=2 if Lb > 0 else 1, stepped=True, cost=450)
                    for r in range((t + 1 + 7) // 8):
                        kcnt = min(8, t + 1 - 8 * r)
                        def fpt(e, r=r, kcnt=kcnt, tp=tp):
                            last = None
                            for kk in range(kcnt):
                                last = e.transpose(
                                    pT[:, 128 * kk:128 * (kk + 1)],
                                    Psb[tp][:, 128 * (8 * r + kk):128 * (8 * r + kk + 1)],
                                    ident[:, :])
                            return last
                        S.add("pe", fpt, reads=["P", "ident"], writes=["pT"],
                              cost=100 + 55 * kcnt)
                        S.add("act", lambda e, r=r, kcnt=kcnt: e.copy(
                            PTs[:, 1024 * r:1024 * r + 128 * kcnt], pT[:, 0:128 * kcnt]),
                            reads=["pT"], writes=["PT"], cost=250 + 107 * kcnt)
                    def fpv(e, t=t):
                        last = None
                        for kk in range(t + 1):
                            last = e.matmul(pS[:, 0:64], PTs[:, 128 * kk:128 * (kk + 1)],
                                            vsb[:, 64 * kk:64 * (kk + 1)],
                                            start=(kk == 0), stop=(kk == t))
                        return last
                    S.add("pe", fpv, reads=["PT"] + [f"v{kk}" for kk in range(t + 1)],
                          writes=["pS"], cost=100 + 55 * (t + 1))
                    S.add("act", lambda e, us=us, tp=tp: e.activation(
                        attn[us][:, :], pS[:, 0:64], AF.Identity, scale=s_rcp[tp][:, :]),
                        reads=["pS", f"s_rcp{tp}"], writes=[f"attn{us}"], cost=300)
                    S.add("pool", lambda e, us=us, u=u, rsl=rsl: e.tensor_tensor(
                        mixed[:, :].rearrange("p (ex d) -> p ex d", ex=16),
                        attn[us][:, :].unsqueeze(1).broadcast_to([128, 16, 64]),
                        rw[rsl][:, 16 * u:16 * (u + 1)].unsqueeze(2).broadcast_to(
                            [128, 16, 64]), ALU.mult),
                        reads=[f"attn{us}", f"rw{rsl}"], writes=["mixed"], cost=2300)
                    def fmt(e):
                        last = None
                        for c in range(8):
                            last = e.transpose(pSb[:, 128 * c:128 * (c + 1)],
                                               mixed[:, 128 * c:128 * (c + 1)],
                                               ident[:, :])
                        return last
                    S.add("pe", fmt, reads=["mixed", "ident"], writes=["pS"], cost=600)
                    S.add("act", lambda e: e.copy(mixT[:, :], pSb[:, 0:1024]),
                          reads=["pS"], writes=["mixT"], cost=1100)
                    for h in range(2):
                        def fo(e, h=h):
                            last = None
                            for c in range(NCH):
                                last = e.matmul(
                                    pC[:, 512 * h:512 * (h + 1)],
                                    mixT[:, 128 * c:128 * (c + 1)],
                                    wo[:, 1024 * c + 512 * h:1024 * c + 512 * (h + 1)],
                                    start=(c == 0), stop=(c == NCH - 1))
                            return last
                        S.add("pe", fo, reads=["mixT", "wo"], writes=["pC" + str(h)],
                              cost=1800)
                        S.add("dve", lambda e, t=t, h=h: e.scalar_tensor_tensor(
                            x[:, 1024 * t + 512 * h:1024 * t + 512 * (h + 1)],
                            pC[:, 512 * h:512 * (h + 1)], 0.5,
                            x[:, 1024 * t + 512 * h:1024 * t + 512 * (h + 1)],
                            ALU.mult, ALU.add),
                            reads=["pC" + str(h), f"x{t}"], writes=[f"x{t}"],
                            cost=750)
                if group_post is not None:
                    group_post(g)

        pT32 = pT[:, :].bitcast(F32)

        def attn_B1_group(g):
            asl = 2 + (g % 2)
            ln_group_stats(g, asl, False, r2a[:, GRP * g:GRP * (g + 1)],
                           b2a[:, GRP * g:GRP * (g + 1)], ["r2a", "b2a"])
            for u in range(GRP):
                t = g * GRP + u
                us = u % 2
                ln_apply(t, u, r2a[:, t:t + 1], b2a[:, t:t + 1], ["r2a", "b2a"])
                transpose8(lns[us], f"lns{us}",
                           (lambda e, u=u: e.copy(lnTa[:, 0:1024], pT[:, :])),
                           ["mixT"])
                pq = pS if us == 0 else pC
                pqk = "pS" if us == 0 else "pC0"
                def fq(e, u=u, pq=pq):
                    last = None
                    for c in range(NCH):
                        last = e.matmul(pq[:, 0:128], wqt[:, 128 * c:128 * (c + 1)],
                                        lnTa[:, 128 * c:128 * (c + 1)],
                                        start=(c == 0), stop=(c == NCH - 1))
                    return last
                S.add("pe", fq, reads=["wqt", "mixT"], writes=[pqk], cost=500)
                S.add("act", lambda e, t=t, pq=pq: e.copy(
                    QTs[:, 128 * t:128 * (t + 1)], pq[:, 0:128]),
                    reads=[pqk], writes=[f"QT{t}"], cost=350)
                klo = 512 * us
                kbk = "pA" + str(us)
                def fk(e, klo=klo):
                    last = None
                    for c in range(NCH):
                        last = e.matmul(pA[:, klo:klo + 128],
                                        lnTa[:, 128 * c:128 * (c + 1)],
                                        wkt[:, 128 * c:128 * (c + 1)],
                                        start=(c == 0), stop=(c == NCH - 1))
                    return last
                S.add("pe", fk, reads=["mixT", "wkt"], writes=[kbk], cost=500)
                S.add("act", lambda e, us=us, klo=klo: e.copy(
                    K2[us][:, :], pA[:, klo:klo + 128]), reads=[kbk],
                    writes=[f"K2{us}"], cost=350)
                for hh, pdst, pdk in ((0, pT32, "pT"), (1, pS, "pS")):
                    def fMh(e, us=us, hh=hh, pdst=pdst):
                        return e.matmul(pdst[:, 0:512], K2[us][:, :],
                                        lns[us][:, 512 * hh:512 * (hh + 1)],
                                        start=True, stop=True)
                    S.add("pe", fMh, reads=[f"K2{us}", f"lns{us}"], writes=[pdk],
                          cost=300)
                    if t == 0:
                        S.add("dve", lambda e, hh=hh, pdst=pdst: e.tensor_copy(
                            Macc[:, 512 * hh:512 * (hh + 1)], pdst[:, 0:512]),
                            reads=[pdk], writes=["Macc"], cost=700)
                    else:
                        S.add("dve", lambda e, hh=hh, pdst=pdst: e.tensor_tensor(
                            Macc[:, 512 * hh:512 * (hh + 1)],
                            Macc[:, 512 * hh:512 * (hh + 1)], pdst[:, 0:512],
                            ALU.add),
                            reads=[pdk, "Macc"], writes=["Macc"], cost=750)

        def attn_B3_group(g):
            for u in range(GRP):
                t = g * GRP + u
                us = u % 2
                jb = Jn[us]
                jk = f"Jn{us}"
                pXa, pna = ((pA, "pA"), (pB, "pB"), (pC, "pC"))[t % 3]
                for h in range(2):
                    S.add("pe", lambda e, t=t, h=h, pXa=pXa: e.matmul(
                        pXa[:, 512 * h:512 * (h + 1)], QTs[:, 128 * t:128 * (t + 1)],
                        Msb[:, 512 * h:512 * (h + 1)], start=True, stop=True),
                        reads=[f"QT{t}", "Msb"], writes=[pna + str(h)], cost=300)
                    S.add("dve", lambda e, t=t, jb=jb, h=h, pXa=pXa:
                          e.scalar_tensor_tensor(
                        jb[:, 512 * h:512 * (h + 1)],
                        x[:, 1024 * t + 512 * h:1024 * t + 512 * (h + 1)],
                        r2a[:, t:t + 1], pXa[:, 512 * h:512 * (h + 1)],
                        ALU.mult, ALU.add),
                        reads=[pna + str(h), f"x{t}", "r2a"], writes=[jk], cost=750)
                def f3(e, stp, us=us, jb=jb):
                    stp(e.bn_stats(bn6[us][:, 0:6], jb[:, 0:512]))
                    stp(e.bn_stats(bn6[us][:, 6:12], jb[:, 512:1024]))
                    stp(e.bn_aggr(mv3[us][:, :], bn6[us][:, 0:12]))
                S.add("dve", f3, reads=[jk], writes=[f"mv3{us}", f"bn6{us}"],
                      n_inc=3, cost=1800)
                def f4(e, stp, us=us):
                    stp(e.activation(s_t3[us][:, 0:1], mv3[us][:, 1:2], AF.Ln,
                                     bias=consts[:, 1:2]))
                    stp(e.activation(s_t3[us][:, 1:2], s_t3[us][:, 0:1], AF.Exp,
                                     scale=-0.5))
                S.add("act", f4, reads=[f"mv3{us}", "ones"], writes=[f"s_t3{us}"],
                      n_inc=2, cost=500)
                S.add("dve", lambda e, us=us: e.scalar_tensor_tensor(
                    s_b3[us][:, :], mv3[us][:, 0:1], -1.0, s_t3[us][:, 1:2],
                    ALU.mult, ALU.mult),
                    reads=[f"mv3{us}", f"s_t3{us}"], writes=[f"s_b3{us}"], cost=250)
                S.add("act", lambda e, us=us, jb=jb: e.activation(
                    jb[:, :], jb[:, :], AF.Identity,
                    scale=s_t3[us][:, 1:2], bias=s_b3[us][:, :]),
                    reads=[jk, f"s_t3{us}", f"s_b3{us}"], writes=[jk], cost=1100)
                S.add("pool", lambda e, jb=jb: e.tensor_tensor(jb[:, :], jb[:, :],
                                                               gmb[:, :], ALU.mult),
                      reads=[jk, "gmb"], writes=[jk], cost=2400)
                S.add("pool", lambda e, jb=jb: e.tensor_tensor(jb[:, :], jb[:, :],
                                                               btb[:, :], ALU.add),
                      reads=[jk, "btb"], writes=[jk], cost=2400)
                S.add("pool", lambda e, t=t, jb=jb: e.tensor_tensor(
                    x[:, 1024 * t:1024 * (t + 1)], x[:, 1024 * t:1024 * (t + 1)],
                    jb[:, :], ALU.add),
                    reads=[f"x{t}", jk], writes=[f"x{t}"], cost=2400)

        for s in range(n_seq):
            if n_sub >= 2:
                emit_ffn(s, second=False,
                         group_post=attn_B1_group if il_b1 else None)
                if not il_b1:
                    for g in range(NT // GRP):
                        attn_B1_group(g)
                S.add("act", lambda e: e.activation(Msb[:, :], Macc[:, :], AF.Identity,
                                                    scale=INV_SQRT_DK),
                      reads=["Macc"], writes=["Msb"], cost=1100)
            elif n_sub >= 1:
                emit_ffn(s, second=False)
            if n_sub >= 3:
                if il_b3:
                    emit_ffn(s, second=True, group_pre=attn_B3_group)
                else:
                    for g in range(NT // GRP):
                        attn_B3_group(g)
                    emit_ffn(s, second=True)
            for t in range(NT):
                S.add("qout", lambda e, t=t, s=s: e.dma_start(
                    xout[T * s + 128 * t:T * s + 128 * (t + 1), :],
                    x[:, 1024 * t:1024 * (t + 1)]), reads=[f"x{t}"],
                    cost=300, xfer=4000)
                if s < n_seq - 1:
                    S.add("qin", lambda e, t=t, s=s: e.dma_start(
                        x[:, 1024 * t:1024 * (t + 1)],
                        xin[T * (s + 1) + 128 * t:T * (s + 1) + 128 * (t + 1), :]),
                        writes=[f"x{t}"], cost=300, xfer=4000)

        dumpables = {
            "rw0": (rw[0][:, :], "rw0"), "lg0": (lg[0][:, :], "lg0"),
            "mv0": (mv[0][:, :], "mv0"), "rstdg0": (rstdg[0][:, :], "rstdg0"),
            "invng0": (invng[0][:, :], "invng0"), "biasg0": (biasg[0][:, :], "biasg0"),
            "lns0": (lns[0][:, :], "lns0"), "qtok0": (qtok[0][:, :], "qtok0"),
            "vsb": (vsb[:, :], "v15"), "attn0": (attn[0][:, :], "attn0"),
            "Psb": (Psb[0][:, :], "P0"), "PTs": (PTs[:, :], "PT"),
            "s_rcp": (s_rcp[0][:, :], "s_rcp0"), "sgb": (sgb[:, :], "sgb"),
            "simn": (simn[:, :], "simn"), "mixed": (mixed[:, :], "mixed"),
            "mixT": (mixT[:, :], "mixT"), "lnT0": (lnT[0][:, :], "lnT0"),
            "qTs": (qTs[:, :], "qT15"), "kTs": (kTs[:, :], "kT15"),
            "s_rsb": (s_rsb[:, :], "rsb"), "g_gt": (g_gt[:, :], "g_gt"),
            "g_ps": (g_ps[:, :], "g_ps"), "prod": (prod[:, :], "prod"),
            "Msb": (Msb[:, :], "Msb"), "QTs": (QTs[:, :], "QT15"),
            "Jn": (Jn[0][:, :], "Jn0"), "r2a": (r2a[:, :], "r2a"),
        }
        for i, nm in enumerate(dump):
            ap, key = dumpables[nm]
            rows = ap.shape[0]
            cols = int(np.prod(ap.shape[1:]))
            base = 4096 + 128 * i
            if ap.dtype == F32:
                S.add("qout", lambda e, ap=ap, base=base, rows=rows, cols=cols:
                      e.dma_start(xout[base:base + rows, 0:cols], ap,
                                  allow_slow_non_contiguous=True), reads=[key])
            else:
                S.add("qw", lambda e, ap=ap, base=base, rows=rows, cols=cols:
                      e.dma_start(xout[base:base + rows, 0:cols], ap,
                                  allow_slow_non_contiguous=True), reads=[key])

        S.schedule()
        streams = S.gen_streams()
        n_stores = (S.cnt["qout0"], S.cnt["qout1"])

        block = st.enter_context(nc.Block())

        def mk(stream, final=False):
            strict = stream in Sched.STRICT

            def run(eng):
                for waits, fn, semh, amt, n_inc, basec, stepped in streams[stream]:
                    for sh, v in waits:
                        eng.wait_ge(sh, v)
                    if strict:
                        eng.wait_ge(semh, basec)
                    if amt == 16 and basec > 0:
                        eng.wait_ge(semh, 16 * basec)
                    if not stepped:
                        inst = fn(eng)
                        inst.then_inc(semh, amt)
                    else:
                        stp = _Step(eng, semh, basec)
                        fn(eng, stp)
                        assert stp.c == basec + n_inc, (stream, stp.c, basec, n_inc)
                if final:
                    eng.wait_ge(sems["qout0"], 16 * n_stores[0])
                    eng.wait_ge(sems["qout1"], 16 * n_stores[1])
            return run

        block.tensor(mk("tensor"))
        block.scalar(mk("scalar"))
        block.vector(mk("vector"))
        block.gpsimd(mk("gpsimd"))
        block.sync(mk("sync", final=True))

    return nc


def _host_inputs(inputs):
    J = np.ascontiguousarray(np.asarray(inputs["L0_attn_J"], dtype=np.float32))
    WQ = np.asarray(inputs["L0_attn_W_Q"], dtype=np.float32)
    WK = np.asarray(inputs["L0_attn_W_K"], dtype=np.float32)
    gamma = np.asarray(inputs["L0_attn_gamma"], dtype=np.float32).reshape(1, D)
    beta = np.asarray(inputs["L0_attn_beta"], dtype=np.float32).reshape(1, D)
    sim = np.asarray(inputs["L0_ffn_sim_matrix"], dtype=np.float32)
    gates = np.asarray(inputs["L0_ffn_gates"], dtype=np.float32).reshape(1, E)
    qp = np.asarray(inputs["L0_ffn_q_proj"], dtype=np.float32)
    kp = np.asarray(inputs["L0_ffn_k_proj"], dtype=np.float32)
    vp = np.asarray(inputs["L0_ffn_v_proj"], dtype=np.float32)
    op = np.asarray(inputs["L0_ffn_o_proj"], dtype=np.float32)

    common = {
        "wq": np.ascontiguousarray(qp.transpose(1, 0, 2).reshape(D, E * HD)),
        "wk": np.ascontiguousarray(kp.transpose(1, 0, 2).reshape(D, E * HD)),
        "wv": np.ascontiguousarray(vp.transpose(1, 0, 2).reshape(D, E * HD)),
        "wo": np.ascontiguousarray(op.reshape(E * HD, D)),
        "wqt": np.ascontiguousarray(WQ.T),
        "wkt": np.ascontiguousarray(WK.T),
        "simr": np.ascontiguousarray(sim.reshape(NCH, 128, E).transpose(1, 0, 2)
                                     .reshape(128, NCH * E)),
        "gates": np.ascontiguousarray(gates),
        "gamma": np.ascontiguousarray(gamma),
        "beta": np.ascontiguousarray(beta),
        "ident": np.eye(128, dtype=BF),
        "trimask": np.where(np.arange(128)[None, :] <= np.arange(128)[:, None],
                            np.float32(0.0), np.float32(-BIGV)).astype(BF),
        "consts": np.broadcast_to(np.array([1.0, LN_EPS, 1.0e-18, 0.0],
                                           dtype=np.float32), (128, 4)).copy(),
    }
    maps = []
    for c in range(NCORES):
        m = dict(common)
        m["xin"] = np.ascontiguousarray(J[SPC * c:SPC * (c + 1)].reshape(SPC * T, D))
        maps.append(m)
    return maps


def kernel(**inputs):
    in_maps = _host_inputs(inputs)
    nc = build_program()
    res = run_bass_kernel_spmd(nc, in_maps, core_ids=list(range(NCORES)))
    out = np.concatenate(
        [np.asarray(r["xout"]).reshape(SPC, T, D) for r in res.results], axis=0)
    return np.ascontiguousarray(out.astype(np.float32))
